# version 01
# baseline (speedup 1.0000x reference)
import numpy as np
import ml_dtypes
import concourse.bass as bass
import concourse.mybir as mybir
from concourse.bass_utils import run_bass_kernel_spmd

F32 = mybir.dt.float32
BF16 = mybir.dt.bfloat16
AF = mybir.ActivationFunctionType
ALU = mybir.AluOpType
AX = mybir.AxisListType
BF = ml_dtypes.bfloat16

T = 2048
D = 1024
NT = T // 128
KC = D // 128
HH = 64
M = 16
E = 17

ENGINES = ("sync", "tensor", "vector", "scalar", "gpsimd")


class _Op:
    __slots__ = ("eng", "fn", "waits", "need_inc", "inc_amt", "tag", "lane")

    def __init__(self, eng, fn, waits, inc_amt, tag, lane):
        self.eng, self.fn, self.waits = eng, fn, waits
        self.need_inc = False
        self.inc_amt = inc_amt
        self.tag = tag
        self.lane = lane


class Tok:
    __slots__ = ("op",)

    def __init__(self, op):
        self.op = op


N_LANES = {"sync": 8, "gpsimd": 4, "scalar": 2, "vector": 2, "tensor": 1}


class Sched:

    def __init__(self, nc):
        self.nc = nc
        self.ops = {e: [] for e in ENGINES}
        self.lane_rr = {e: 0 for e in ENGINES}

    def emit(self, eng, fn, after=(), dma=False, tag=""):
        lane = None
        if dma:
            i = self.lane_rr[eng]
            self.lane_rr[eng] = (i + 1) % N_LANES[eng]
            lane = (eng, i)
        waits = []
        for t in after:
            if t is None:
                continue
            if t.op.eng == eng and t.op.lane is None:
                continue
            waits.append(t)
            t.op.need_inc = True
        op = _Op(eng, fn, waits, 16 if dma else 1, tag, lane)
        if dma:
            op.need_inc = True
        self.ops[eng].append(op)
        return Tok(op)

    def last(self, eng):
        return Tok(self.ops[eng][-1]) if self.ops[eng] else None

    def lasts_all(self):
        out = []
        for e in ENGINES:
            seen_lanes = set()
            for op in reversed(self.ops[e]):
                if op.lane is None:
                    out.append(Tok(op))
                    break
            for op in reversed(self.ops[e]):
                if op.lane is not None and op.lane not in seen_lanes:
                    seen_lanes.add(op.lane)
                    out.append(Tok(op))
        return out

    def barrier(self):
        lasts = self.lasts_all()
        toks = []
        for e in ENGINES:
            toks.append(self.emit(e, lambda h: h.nop(), after=lasts,
                                  tag="barrier"))
        return toks

    def finalize(self, final_waits):
        nc = self.nc
        for t in final_waits:
            t.op.need_inc = True
        val = {}
        cnt = {}
        for e in ENGINES:
            for op in self.ops[e]:
                key = op.lane if op.lane is not None else e
                if op.need_inc or op.lane is not None:
                    cnt[key] = cnt.get(key, 0) + op.inc_amt
                val[op] = cnt.get(key, 0)
        import contextlib
        stack = contextlib.ExitStack()
        sems = {}
        for e in ENGINES:
            sems[e] = stack.enter_context(nc.semaphore(f"sem_{e}"))
            for i in range(N_LANES[e]):
                sems[(e, i)] = stack.enter_context(nc.semaphore(f"dl_{e}{i}"))

        def run(h, eng, finals=()):
            waited = {}
            lane_issued = {}
            for op in self.ops[eng]:
                for t in op.waits:
                    key = t.op.lane if t.op.lane is not None else t.op.eng
                    v = val[t.op]
                    if waited.get(key, 0) < v:
                        h.wait_ge(sems[key], v)
                        waited[key] = v
                if op.lane is not None:
                    prior = lane_issued.get(op.lane, 0)
                    if prior and waited.get(op.lane, 0) < prior:
                        h.wait_ge(sems[op.lane], prior)
                        waited[op.lane] = prior
                    lane_issued[op.lane] = val[op]
                inst = op.fn(h)
                if op.need_inc or op.lane is not None:
                    key = op.lane if op.lane is not None else eng
                    if inst is None:
                        h.nop().then_inc(sems[key], op.inc_amt)
                    else:
                        inst.then_inc(sems[key], op.inc_amt)
            for t in finals:
                key = t.op.lane if t.op.lane is not None else t.op.eng
                v = val[t.op]
                if waited.get(key, 0) < v:
                    h.wait_ge(sems[key], v)
                    waited[key] = v

        with nc.Block() as block:
            @block.sync
            def _(h):
                run(h, "sync", finals=final_waits)

            @block.tensor
            def _(h):
                run(h, "tensor")

            @block.vector
            def _(h):
                run(h, "vector")

            @block.scalar
            def _(h):
                run(h, "scalar")

            @block.gpsimd
            def _(h):
                run(h, "gpsimd")
        stack.close()


def prep_host(inputs):
    f32 = np.float32
    ws = {}

    def c(x):
        return np.ascontiguousarray(x, dtype=f32)

    ws["norm_w_rep"] = c(np.broadcast_to(inputs["L0_attn_norm_w"], (128, D)))
    ws["WvT"] = c(inputs["L0_attn_W_v"].T)
    ws["WaT"] = c(inputs["L0_attn_W_a"].T)
    ws["WoutT"] = c(inputs["L0_attn_W_out"].T)
    ws["We"] = c(inputs["L0_ffn_We"])
    ws["be_cols"] = c(inputs["L0_ffn_be"].reshape(20, 8, 128).transpose(0, 2, 1))
    ws["Wq0"] = c(inputs["L0_ffn_Wq"])
    ws["Wk0"] = c(inputs["L0_ffn_Wk"])
    ws["Wv0"] = c(inputs["L0_ffn_Wv"])
    ws["Wo0"] = c(inputs["L0_ffn_Wo"])
    ws["ew_rep"] = c(np.broadcast_to(inputs["L0_ffn_ew"].reshape(1, 18), (128, 18)))
    ws["actw_rep"] = c(np.broadcast_to(inputs["L0_ffn_act_w"].reshape(1, 6), (128, 6)))
    ws["ln_g_rep"] = c(np.broadcast_to(inputs["L0_ffn_ln_g"], (128, D)))
    ws["ln_b_rep"] = c(np.broadcast_to(inputs["L0_ffn_ln_b"], (128, D)))
    for nm, bn in (("Wq", "bq"), ("Wk", "bk"), ("Wv", "bv")):
        w = inputs[f"L1_attn_{nm}"]
        ws[nm + "1"] = c(w.transpose(1, 0, 2).reshape(D, D))
        bias = inputs[f"L1_attn_{bn}"].reshape(D)
        ws[bn + "1"] = c(bias.reshape(8, 128).T)
    ws["Wo1"] = c(inputs["L1_attn_Wo"])
    ws["bo1_rep"] = c(np.broadcast_to(inputs["L1_attn_bo"], (128, D)))
    rm = inputs["L1_ffn_rail_maker"]
    Wyp = np.zeros((4, 8, 128, 128), f32)
    for h in range(4):
        for r in range(64):
            rc, ro = r // 8, r % 8
            Wyp[h, rc, ro*16:(ro+1)*16, ro*16:(ro+1)*16] = rm[h, r]
    ws["Wyp"] = Wyp
    ru = inputs["L1_ffn_rail_up_proj"]
    Wzp = np.zeros((64, 128, 128), f32)
    for h in range(4):
        for p in range(32):
            t, base = ZMAP[(h, p)]
            q = p // 2
            for rr in range(2):
                r = 2 * p + rr
                row0 = base + (r - 4 * q) * 16
                Wzp[t, row0:row0 + 16, rr * 64:(rr + 1) * 64] = ru[h, r].T
    ws["Wzp"] = Wzp
    ws["WcT"] = c(inputs["L1_ffn_w_cproj"].T)
    ws["ident_bf"] = np.eye(128, dtype=f32).astype(BF)
    ws["ident_f32"] = np.eye(128, dtype=f32)
    mask = np.zeros((128, 128), f32)
    for i in range(128):
        mask[i, i + 1:] = -1e9
    ws["mask128"] = mask
    return ws


class B:
    def __init__(self, dbg_taps=(), phases=("p1", "p2", "p3", "p4")):
        self.nc = bass.Bass("TRN2", target_bir_lowering=False)
        self.s = Sched(self.nc)
        import contextlib
        self.stack = contextlib.ExitStack()
        self.phase_stack = None
        self.dbg_taps = set(dbg_taps)
        self.dbg_tokens = []
        self.phases = phases
        self.out_toks = []
        self.psum = [self.stack.enter_context(
            self.nc.psum_tensor(f"psb{i}", [128, 512], F32)) for i in range(8)]
        self.bank_prod = [None] * 8
        self.bank_cons = [None] * 8
        self._rr = 0

    def din(self, name, shape, dtype=F32):
        return self.nc.dram_tensor(name, list(shape), dtype, kind="ExternalInput").ap()

    def dout(self, name, shape, dtype=F32):
        return self.nc.dram_tensor(name, list(shape), dtype, kind="ExternalOutput").ap()

    def dscratch(self, name, shape, dtype):
        return self.nc.dram_tensor(name, list(shape), dtype, kind="Internal").ap()

    def sbuf(self, name, shape, dtype, phase=False):
        st = self.phase_stack if phase else self.stack
        return st.enter_context(self.nc.sbuf_tensor(name, list(shape), dtype))

    def open_phase(self):
        import contextlib
        assert self.phase_stack is None
        self.phase_stack = contextlib.ExitStack()

    def close_phase(self):
        self.phase_stack.close()
        self.phase_stack = None
        self.s.barrier()

    def bank(self):
        b = self._rr
        self._rr = (self._rr + 1) % 8
        return b

    def mm_acc(self, bankid, pieces, out_ap=None, tag="mm"):
        n = len(pieces)
        tok = None
        for i, (lf, rf, aft) in enumerate(pieces):
            start, stop = i == 0, i == n - 1
            waits = list(aft)
            if start and self.bank_cons[bankid] is not None:
                waits.append(self.bank_cons[bankid])
            oa = out_ap if out_ap is not None else self.psum[bankid][:]
            tok = self.s.emit("tensor", lambda h, lf=lf, rf=rf, s0=start, s1=stop,
                              oa=oa: h.matmul(oa, lf(), rf(), start=s0, stop=s1),
                              after=waits, tag=tag)
        self.bank_prod[bankid] = tok
        return tok

    def consume(self, eng, bankid, fn, after=(), tag="evac"):
        waits = list(after)
        if self.bank_prod[bankid] is not None:
            waits.append(self.bank_prod[bankid])
        tok = self.s.emit(eng, fn, after=waits, tag=tag)
        self.bank_cons[bankid] = tok
        return tok

    def transpose_group(self, bankid, pieces, bf16=True, tag="tr"):
        tok = None
        first = True
        for of, inf, aft in pieces:
            waits = list(aft)
            if first and self.bank_cons[bankid] is not None:
                waits.append(self.bank_cons[bankid])
            first = False
            ba = self.psum[bankid][:].bitcast(BF16) if bf16 else self.psum[bankid][:]
            tok = self.s.emit("tensor", lambda h, of=of, inf=inf, ba=ba:
                              h.transpose(of(ba), inf(), self.ident()),
                              after=waits, tag=tag)
        self.bank_prod[bankid] = tok
        return tok

    def ident(self):
        return self.ident_sb[:]

    def identf(self):
        return self.identf_sb[:]

    def dbg(self, name, ap_fn, shape, dtype, after):
        if name not in self.dbg_taps:
            return
        o = self.dout("dbg_" + name, shape, dtype)
        self.dbg_tokens.append(self.s.emit(
            "sync", lambda h: h.dma_start(out=o, in_=ap_fn()),
            after=after, dma=True, tag="dbg_" + name))


def tm_to_fm(b, src_ap_fn, nchunks, dest_fn, after, src_bf16, evac_eng="scalar"):
    toks = []
    per = 8 if src_bf16 else 4
    i = 0
    while i < nchunks:
        cnt = min(per, nchunks - i)
        bk = b.bank()
        pieces = []
        for c in range(i, i + cnt):
            def of(ba, c=c, i=i, src_bf16=src_bf16):
                if src_bf16:
                    return ba.rearrange("p (n m) -> p n m", m=128)[:, c - i, :]
                return None
            if src_bf16:
                pieces.append((of, (lambda c=c: src_ap_fn(c)), after))
            else:
                pieces.append((None, None, None))
        if src_bf16:
            tok_p = b.transpose_group(bk, pieces, bf16=True)
            def ev(h, bk=bk, i=i, cnt=cnt):
                return h.copy(dest_fn(i, cnt),
                              b.psum[bk][:].bitcast(BF16).rearrange(
                                  "p (n m) -> p n m", m=128)[:, 0:cnt, :])
            toks.append(b.consume(evac_eng, bk, ev))
        else:
            first = True
            tok_p = None
            for c in range(i, i + cnt):
                waits = list(after)
                if first and b.bank_cons[bk] is not None:
                    waits.append(b.bank_cons[bk])
                first = False
                tok_p = b.s.emit("tensor", lambda h, c=c, i=i, bk=bk:
                                 h.transpose(b.psum[bk][:].rearrange(
                                     "p (n m) -> p n m", m=128)[:, c - i, :],
                                     src_ap_fn(c), b.identf()),
                                 after=waits, tag="trf")
            b.bank_prod[bk] = tok_p
            def ev(h, bk=bk, i=i, cnt=cnt):
                return h.copy(dest_fn(i, cnt),
                              b.psum[bk][:].rearrange(
                                  "p (n m) -> p n m", m=128)[:, 0:cnt, :])
            toks.append(b.consume(evac_eng, bk, ev))
        i += cnt
    return toks


def build(dbg_taps=(), phases=("p1", "p2", "p3", "p4")):
    b = B(dbg_taps, phases)
    nc, s = b.nc, b.s

    x_in = b.din("x", (T, D))
    y_out = b.dout("y", (T, D))
    g = {}
    for name, shape, dt in (
        ("norm_w_rep", (128, D), F32), ("WvT", (D, D), F32),
        ("WaT", (D, HH * M * E), F32), ("WoutT", (D, D), F32),
        ("We", (20, D, D), F32), ("be_cols", (20, 128, 8), F32),
        ("Wq0", (6, D, D), F32), ("Wk0", (6, D, D), F32),
        ("Wv0", (6, D, D), F32), ("Wo0", (6, D, D), F32),
        ("ew_rep", (128, 18), F32), ("actw_rep", (128, 6), F32),
        ("ln_g_rep", (128, D), F32), ("ln_b_rep", (128, D), F32),
        ("Wq1", (D, D), F32), ("Wk1", (D, D), F32), ("Wv1", (D, D), F32),
        ("bq1", (128, 8), F32), ("bk1", (128, 8), F32), ("bv1", (128, 8), F32),
        ("Wo1", (D, D), F32), ("bo1_rep", (128, D), F32),
        ("Wyp", (4, 8, 128, 128), F32), ("Wzp", (64, 128, 128), F32),
        ("WcT", (16384, D), F32),
        ("ident_bf", (128, 128), BF16), ("ident_f32", (128, 128), F32),
        ("mask128", (128, 128), F32),
    ):
        g[name] = b.din(name, shape, dt)

    _ds = b.dout if "spill" in b.dbg_taps else b.dscratch
    x_spill = _ds("x_spill", (NT, 128, D), F32)
    exp_spill = _ds("exp_spill", (NT, 128, HH * M * E), BF16)
    u_spill = _ds("u_spill", (NT, 128, D), F32)
    rc_spill = _ds("rc_spill", (NT, 128, D), F32)
    node_spill = _ds("node_spill", (7, KC, 128, T), BF16)
    q_spill = _ds("q_spill", (KC, 128, T), F32)
    h_spill = b.dscratch("h_spill", (NT, 128, 128, 128), BF16)

    b.ident_sb = b.sbuf("ident_sb", [128, 128], BF16)
    b.identf_sb = b.sbuf("identf_sb", [128, 128], F32)
    xt_sb = [b.sbuf(f"xt{i}", [128, D], F32) for i in range(2)]
    stat = b.sbuf("stat", [128, 8], F32)
    tmb = b.sbuf("tmb", [128, D], BF16)

    consts = []
    for dst, src in ((b.ident_sb, "ident_bf"), (b.identf_sb, "ident_f32")):
        consts.append(s.emit("sync", lambda h, d=dst, sr=src:
                             h.dma_start(out=d[:], in_=g[sr]),
                             dma=True, tag="const"))
    ctok = consts[-1]
    b.consts = consts

    b._ln_prev = None

    def spine_ln(src_ap, dst_ap, after, eps=1e-5, tag="ln", scratch_guard=()):
        aft0 = list(after) + list(scratch_guard) + [b._ln_prev]
        t1 = s.emit("scalar", lambda h: h.activation(
            tmb[:], src_ap, AF.Square, accum_out=stat[:, 0:1]),
            after=aft0, tag=tag + "_sq")
        t2 = s.emit("scalar", lambda h: h.activation(
            tmb[:], src_ap, AF.Copy, accum_out=stat[:, 1:2]),
            after=aft0, tag=tag + "_sum")
        t3 = s.emit("vector", lambda h: h.tensor_scalar(
            stat[:, 2:3], stat[:, 1:2], 1.0 / D, None, op0=ALU.mult),
            after=[t2], tag=tag + "_mu")
        t4 = s.emit("scalar", lambda h: h.activation(
            stat[:, 3:4], stat[:, 2:3], AF.Square), after=[t3], tag=tag + "_mu2")
        t5 = s.emit("vector", lambda h: h.tensor_scalar(
            stat[:, 4:5], stat[:, 0:1], 1.0 / D, None, op0=ALU.mult),
            after=[t1], tag=tag + "_ex2")
        t5 = s.emit("vector", lambda h: h.drain(), after=[t5], tag=tag + "_d5")
        t6 = s.emit("vector", lambda h: h.tensor_tensor(
            stat[:, 5:6], stat[:, 4:5], stat[:, 3:4], op=ALU.subtract),
            after=[t4, t5], tag=tag + "_var")
        t6 = s.emit("vector", lambda h: h.drain(), after=[t6], tag=tag + "_d6")
        t7 = s.emit("vector", lambda h: h.tensor_scalar(
            stat[:, 5:6], stat[:, 5:6], 1.0, eps, op0=ALU.mult, op1=ALU.add),
            after=[t6], tag=tag + "_vare")
        t8 = s.emit("scalar", lambda h: h.activation(
            stat[:, 6:7], stat[:, 5:6], AF.Ln), after=[t7], tag=tag + "_ln")
        t9 = s.emit("scalar", lambda h: h.activation(
            stat[:, 7:8], stat[:, 6:7], AF.Exp, scale=-0.5),
            after=[t8], tag=tag + "_rstd")
        t10 = s.emit("vector", lambda h: h.tensor_scalar(
            dst_ap, src_ap, stat[:, 2:3], stat[:, 7:8],
            op0=ALU.subtract, op1=ALU.mult), after=[t9], tag=tag + "_apply")
        b._ln_prev = t10
        return t10

    b.spine_ln = spine_ln
    ctx = dict(b=b, s=s, g=g, x_in=x_in, y_out=y_out, xt_sb=xt_sb,
               stat=stat, tmb=tmb, ctok=ctok, x_spill=x_spill,
               exp_spill=exp_spill, u_spill=u_spill, rc_spill=rc_spill,
               node_spill=node_spill, q_spill=q_spill, h_spill=h_spill)

    if "p1" in phases:
        p1(ctx)
    for ph in phases:
        if ph.startswith("p2n"):
            ctx["p2_nodes"] = int(ph[3:])
        if ph == "p2bar":
            ctx["p2_bar"] = True
    if "p2" in phases or any(ph.startswith("p2n") for ph in phases) or "p2bar" in phases:
        p2(ctx)
    if "p3" in phases:
        p3(ctx)
    if "p4" in phases:
        p4(ctx)

    finals = list(b.dbg_tokens) + b.out_toks
    if not finals:
        finals = [ctok]
    s.finalize(finals)
    return b, ctx


def p1(ctx):
    b = ctx["b"]; s = ctx["s"]; g = ctx["g"]
    xt_sb = ctx["xt_sb"]; stat = ctx["stat"]
    tmb = ctx["tmb"]
    x_in = ctx["x_in"]; ctok = ctx["ctok"]
    exp_spill = ctx["exp_spill"]; u_spill = ctx["u_spill"]
    rc_spill = ctx["rc_spill"]; x_spill = ctx["x_spill"]
    RG = 272 * 8
    EPS = 1e-5

    b.open_phase()
    hfm = b.sbuf("hfm", [128, KC, T], BF16, phase=True)
    normw_sb = b.sbuf("normw_sb", [128, D], F32, phase=True)
    wv_sb = b.sbuf("wv_sb", [128, KC, D], BF16, phase=True)
    wa_sb = [b.sbuf(f"wa{i}", [128, KC, RG], BF16, phase=True) for i in range(2)]
    logit = b.sbuf("logit", [128, RG], F32, phase=True)
    expb = b.sbuf("expb", [128, RG], BF16, phase=True)
    nmax = b.sbuf("nmax", [128, 128], F32, phase=True)
    denom = b.sbuf("denom", [128, 128], F32, phase=True)
    recip = b.sbuf("recip", [128, 128], F32, phase=True)
    uh = b.sbuf("uh", [128, 128], F32, phase=True)
    vh_sb = b.sbuf("vh_sb", [128, 128], BF16, phase=True)
    x3 = x_in.rearrange("(n p) d -> n p d", p=128)
    tok_nw = s.emit("sync", lambda h: h.dma_start(
        out=normw_sb[:], in_=g["norm_w_rep"]), dma=True, tag="nwload")

    tile_busy = [None, None]
    tmb_busy = None
    for t in range(NT):
        xt = xt_sb[t % 2]
        tl = s.emit("sync", lambda h, t=t, xt=xt: h.dma_start(
            out=xt[:], in_=x3[t]), after=[tile_busy[t % 2]], dma=True,
            tag=f"xload{t}")
        t_sq = s.emit("scalar", lambda h, xt=xt: h.activation(
            tmb[:], xt[:], AF.Square, accum_out=stat[:, 0:1]),
            after=[tl, tmb_busy], tag="sq")
        t_v = s.emit("vector", lambda h: h.tensor_scalar(
            stat[:, 1:2], stat[:, 0:1], 1.0 / D, EPS, op0=ALU.mult, op1=ALU.add),
            after=[t_sq], tag="rmsv")
        t_l = s.emit("scalar", lambda h: h.activation(
            stat[:, 2:3], stat[:, 1:2], AF.Ln), after=[t_v], tag="rmsln")
        t_r = s.emit("scalar", lambda h: h.activation(
            stat[:, 3:4], stat[:, 2:3], AF.Exp, scale=-0.5),
            after=[t_l], tag="rmsr")
        t_h = s.emit("vector", lambda h, xt=xt: h.scalar_tensor_tensor(
            tmb[:], xt[:], stat[:, 3:4], normw_sb[:],
            op0=ALU.mult, op1=ALU.mult), after=[t_r, tok_nw, ctok], tag="happly")
        tile_busy[t % 2] = t_h
        bk = b.bank()
        pieces = [((lambda ba, c=c: ba.rearrange("p (n m) -> p n m", m=128)[:, c, :]),
                   (lambda c=c: tmb[:, c * 128:(c + 1) * 128]), [t_h, ctok])
                  for c in range(KC)]
        tmb_busy = b.transpose_group(bk, pieces, bf16=True, tag="htr")
        te = b.consume("scalar", bk, lambda h, bk=bk, t=t: h.copy(
            hfm[:, :, t * 128:(t + 1) * 128],
            b.psum[bk][:].bitcast(BF16).rearrange("p (n m) -> p n m", m=128)),
            tag="hevac")
        ctx.setdefault("tok_hfm", [None] * NT)[t] = te

    tok_hfm = ctx["tok_hfm"]
    b.dbg("hfm", lambda: hfm[:], (128, KC, T), BF16, tok_hfm)

    tok_wv = []
    for k in range(KC):
        tok_wv.append(s.emit("gpsimd", lambda h, k=k: h.dma_start(
            out=wv_sb[:, k, :], in_=g["WvT"][k * 128:(k + 1) * 128, :]),
            dma=True, tag="wvload"))

    tok_wa = [[None] * KC for _ in range(2)]
    wa_free = [None, None]
    prev_sp = [None, None, None]
    prev_tmax = None
    prev_tu1 = None
    for rg in range(8):
        buf = rg % 2
        for k in range(KC):
            tok_wa[buf][k] = s.emit("gpsimd", lambda h, k=k, rg=rg, buf=buf:
                                    h.dma_start(out=wa_sb[buf][:, k, :],
                                                in_=g["WaT"][k * 128:(k + 1) * 128,
                                                             rg * RG:(rg + 1) * RG]),
                                    after=[wa_free[buf]], dma=True,
                                    tag=f"waload{rg}")
        for t in range(NT):
            mm_toks = []
            for rc in range(8):
                bk = rc
                pieces = []
                for k in range(KC):
                    pieces.append((
                        (lambda k=k, t=t: hfm[:, k, t * 128:(t + 1) * 128]),
                        (lambda k=k, rc=rc, buf=buf: wa_sb[buf][:, k, rc * 272:(rc + 1) * 272]),
                        [tok_hfm[t], tok_wa[buf][k]]))
                b.mm_acc(bk, pieces, out_ap=b.psum[bk][:, 0:272], tag="gamm")
                mm_toks.append(b.bank_prod[bk])
            wa_free[buf] = mm_toks[-1]
            ev_toks = []
            for rc in range(8):
                aft = [prev_tmax] if rc == 0 else []
                ev_toks.append(b.consume("scalar", rc, lambda h, rc=rc:
                                         h.copy(logit[:, rc * 272:(rc + 1) * 272],
                                                b.psum[rc][:, 0:272]),
                                         after=aft, tag="gaevac"))
            l3 = logit[:].rearrange("p (gg e) -> p gg e", e=E)
            t_max = s.emit("vector", lambda h, l3=l3: h.reduce_max(
                nmax[:].rearrange("p (gg o) -> p gg o", o=1), l3, axis=AX.X,
                negate=True), after=ev_toks, tag="gamax")
            prev_tmax = t_max
            nb = nmax[:].rearrange("p (gg o) -> p gg o", o=1).broadcast_to([128, 128, E])
            t_sub = s.emit("vector", lambda h, l3=l3, nb=nb: h.tensor_tensor(
                expb[:].rearrange("p (gg e) -> p gg e", e=E), l3, nb, op=ALU.add),
                after=[t_max, prev_sp[0]], tag="gasub")
            t_exp = s.emit("scalar", lambda h: h.activation(
                expb[:], expb[:], AF.Exp), after=[t_sub], tag="gaexp")
            t_den = s.emit("vector", lambda h: h.reduce_sum(
                denom[:].rearrange("p (gg o) -> p gg o", o=1),
                expb[:].rearrange("p (gg e) -> p gg e", e=E), axis=AX.X),
                after=[t_exp], tag="gaden")
            t_den = s.emit("vector", lambda h: h.drain(), after=[t_den], tag="gadr")
            t_rec = s.emit("vector", lambda h: h.reciprocal(
                recip[:], denom[:]), after=[t_den, prev_sp[2]], tag="garec")
            vpieces = [((lambda k=k, t=t: hfm[:, k, t * 128:(t + 1) * 128]),
                        (lambda k=k, rg=rg: wv_sb[:, k, rg * 128:(rg + 1) * 128]),
                        [tok_hfm[t], tok_wv[k]]) for k in range(KC)]
            b.mm_acc(0, vpieces, out_ap=b.psum[0][:, 0:128], tag="vmm")
            t_v = b.consume("scalar", 0, lambda h: h.copy(
                vh_sb[:], b.psum[0][:, 0:128]), after=[prev_tu1], tag="vevac")
            t_u1 = s.emit("vector", lambda h, rg=rg, t=t: h.tensor_tensor(
                uh[:], expb[:, 0::E], vh_sb[:],
                op=ALU.mult), after=[t_exp, t_v, prev_sp[1]], tag="gau1")
            t_u2 = s.emit("vector", lambda h: h.tensor_tensor(
                uh[:], uh[:], recip[:], op=ALU.mult), after=[t_rec, t_u1], tag="gau2")
            sp1 = s.emit("sync", lambda h, t=t, rg=rg: h.dma_start(
                out=exp_spill[t, :, rg * RG:(rg + 1) * RG], in_=expb[:]),
                after=[t_exp], dma=True, tag="spill_exp")
            sp2 = s.emit("sync", lambda h, t=t, rg=rg: h.dma_start(
                out=u_spill[t, :, rg * 128:(rg + 1) * 128], in_=uh[:]),
                after=[t_u2], dma=True, tag="spill_u")
            sp3 = s.emit("sync", lambda h, t=t, rg=rg: h.dma_start(
                out=rc_spill[t, :, rg * 128:(rg + 1) * 128], in_=recip[:]),
                after=[t_rec], dma=True, tag="spill_rc")
            prev_sp = [sp1, sp2, sp3]
            prev_tu1 = t_u1
    b.close_phase()

    b.open_phase()
    A_sc_h = b.sbuf("A_sc", [64, 256, 128], BF16, phase=True)
    A_sc = A_sc_h[:]
    u_sc = b.sbuf("u_sc", [64, 16, 128], F32, phase=True)[:]
    r_sc = b.sbuf("r_sc", [64, 16, 128], F32, phase=True)[:]
    y_sc = b.sbuf("y_sc", [64, 16, 128], F32, phase=True)[:]
    prod_h = b.sbuf("prod", [64, 256], F32, phase=True)
    prod = prod_h[:]
    praw = b.sbuf("praw", [64, 16], F32, phase=True)[:]
    praw2 = b.sbuf("praw2", [64, 16], F32, phase=True)[:]
    st_sb = b.sbuf("st_sb", [64, 16], F32, phase=True)[:]
    expin = [b.sbuf("expin0", [128, HH * M * E], BF16, phase=True)]
    uin = [b.sbuf("uin0", [128, D], F32, phase=True)]
    rcin = [b.sbuf("rcin0", [128, D], F32, phase=True)]
    ytm = b.sbuf("ytm", [128, D], BF16, phase=True)
    yfmt = b.sbuf("yfmt", [128, KC, 128], BF16, phase=True)
    wout_sb = b.sbuf("wout_sb", [128, KC, D], BF16, phase=True)
    x1t = b.sbuf("x1t", [128, D], F32, phase=True)
    x1fmt = b.sbuf("x1fmt", [128, KC, 128], BF16, phase=True)
    node_spill = ctx["node_spill"]

    tok_wout = []
    for k in range(KC):
        tok_wout.append(s.emit("gpsimd", lambda h, k=k: h.dma_start(
            out=wout_sb[:, k, :], in_=g["WoutT"][k * 128:(k + 1) * 128, :]),
            dma=True, tag="woutload"))

    t_zero = s.emit("vector", lambda h: h.memset(st_sb, 0.0), tag="szero")

    in_busy = [None]
    xt_busy = [None, None]
    x1t_busy = []
    tmb_busy = None
    tok_x1spill = []
    prev_step = t_zero
    tok_xfm = [None] * NT
    dbg_x1 = b.dout("dbg_x1", (NT, 128, D)) if "x1" in b.dbg_taps else None
    for t in range(NT):
        bufi = 0
        li1 = s.emit("sync", lambda h, t=t, bufi=bufi: h.dma_start(
            out=expin[bufi][:], in_=exp_spill[t]), after=[in_busy[bufi]],
            dma=True, tag="ld_exp")
        li2 = s.emit("sync", lambda h, t=t, bufi=bufi: h.dma_start(
            out=uin[bufi][:], in_=u_spill[t]), after=[in_busy[bufi]],
            dma=True, tag="ld_u")
        li3 = s.emit("sync", lambda h, t=t, bufi=bufi: h.dma_start(
            out=rcin[bufi][:], in_=rc_spill[t]), after=[in_busy[bufi]],
            dma=True, tag="ld_rc")
        prep_toks = []
        e4 = lambda bufi=bufi: expin[bufi][:].rearrange(
            "p (hh r) -> p hh r", r=272)
        for i in range(M):
            for jg in range(2):
                bk = b.bank()
                pieces = []
                for jj in range(8):
                    j = jg * 8 + jj
                    def of(ba, jj=jj):
                        return ba.rearrange("p (n m) -> p n m", m=128)[0:64, jj, :]
                    def inf(i=i, j=j, bufi=bufi):
                        col = i * E + (j + 1)
                        return expin[bufi][:].rearrange(
                            "p (hh r) -> p hh r", r=272)[:, :, col]
                    pieces.append((of, inf, [li1, prev_step]))
                b.transpose_group(bk, pieces, bf16=True, tag="Atr")
                def ev(h, bk=bk, i=i, jg=jg):
                    return h.copy(
                        A_sc[:, i * 16 + jg * 8:i * 16 + jg * 8 + 8, :],
                        b.psum[bk][:].bitcast(BF16).rearrange(
                            "p (n m) -> p n m", m=128)[0:64, :, :])
                prep_toks.append(b.consume("scalar", bk, ev, tag="Aevac"))
        for which, src, dst, ltok in ((0, uin, u_sc, li2), (1, rcin, r_sc, li3)):
            for grp in range(4):
                bk = b.bank()
                first = True
                tok_p = None
                for ii in range(4):
                    i = grp * 4 + ii
                    waits = [ltok, prev_step]
                    if first and b.bank_cons[bk] is not None:
                        waits.append(b.bank_cons[bk])
                    first = False
                    tok_p = s.emit(
                        "tensor", lambda h, i=i, ii=ii, bk=bk, bufi=bufi, src=src:
                        h.transpose(b.psum[bk][:].rearrange(
                            "p (n m) -> p n m", m=128)[0:64, ii, :],
                            src[bufi][:].rearrange(
                                "p (hh i) -> p hh i", i=16)[:, :, i],
                            b.identf()), after=waits, tag="utr")
                b.bank_prod[bk] = tok_p
                prep_toks.append(b.consume(
                    "scalar", bk, lambda h, bk=bk, grp=grp, dst=dst:
                    h.copy(dst[:, grp * 4:grp * 4 + 4, :],
                           b.psum[bk][:].rearrange(
                               "p (n m) -> p n m", m=128)[0:64]),
                    tag="uevac"))
        for tt in range(128):
            if tt == 0:
                in1_fn = lambda: st_sb.unsqueeze(1).broadcast_to([64, 16, 16])
                aft_m = prep_toks + [prev_step]
            else:
                in1_fn = (lambda tt=tt: y_sc[:, :, tt - 1]
                          .unsqueeze(1).broadcast_to([64, 16, 16]))
                aft_m = [prev_step]
            t_m = s.emit("vector", lambda h, tt=tt, in1_fn=in1_fn: h.tensor_tensor(
                prod.rearrange("p (a bb) -> p a bb", bb=16),
                A_sc[:, :, tt].rearrange("p (a bb) -> p a bb", bb=16),
                in1_fn(), op=ALU.mult), after=aft_m, tag="scm")
            t_r2 = s.emit("vector", lambda h: h.reduce_sum(
                praw.unsqueeze(2),
                prod.rearrange("p (a bb) -> p a bb", bb=16), axis=AX.X),
                after=[t_m], tag="scr")
            t_r2 = s.emit("vector", lambda h: h.drain(), after=[t_r2], tag="scdr")
            t_f1 = s.emit("vector", lambda h, tt=tt: h.tensor_tensor(
                praw2, praw, r_sc[:, :, tt], op=ALU.mult),
                after=[t_r2], tag="scf1")
            t_f1 = s.emit("vector", lambda h: h.drain(), after=[t_f1], tag="scdr2")
            t_f2 = s.emit("vector", lambda h, tt=tt: h.tensor_tensor(
                y_sc[:, :, tt], praw2, u_sc[:, :, tt], op=ALU.add),
                after=[t_f1], tag="scf2")
            t_f2 = s.emit("vector", lambda h: h.drain(), after=[t_f2], tag="scdr3")
            prev_step = t_f2
            if t == 0 and tt == 1 and "scanstep" in b.dbg_taps:
                for nm, ap_, shp in (("prod1", prod, (64, 256)),
                                     ("praw1", praw, (64, 16)),
                                     ("praw2d", praw2, (64, 16))):
                    o_ = b.dout("dbg_" + nm, shp, F32)
                    tk = s.emit("sync", lambda h, o_=o_, ap_=ap_: h.dma_start(
                        out=o_, in_=ap_), after=[t_f2], dma=True, tag="dbgss")
                    b.dbg_tokens.append(tk)
                prev_step = b.dbg_tokens[-1]
        prev_step = s.emit("vector", lambda h: h.tensor_copy(
            st_sb, y_sc[:, :, 127]), after=[prev_step], tag="scstash")
        prev_step = s.emit("vector", lambda h: h.drain(), after=[prev_step], tag="scdr4")
        in_busy[bufi] = prev_step
        scan_done = prev_step
        if t == 0 and "scan0" in b.dbg_taps:
            for nm, ap_, shp, dt_ in (("A0", A_sc, (64, 256, 128), BF16),
                                      ("u0", u_sc, (64, 16, 128), F32),
                                      ("r0", r_sc, (64, 16, 128), F32),
                                      ("y0", y_sc, (64, 16, 128), F32)):
                o_ = b.dout("dbg_" + nm, shp, dt_)
                b.dbg_tokens.append(s.emit(
                    "sync", lambda h, o_=o_, ap_=ap_: h.dma_start(out=o_, in_=ap_),
                    after=[scan_done], dma=True, tag="dbg" + nm))
            in_busy[bufi] = b.dbg_tokens[-1]
            prev_step = b.dbg_tokens[-1]

        ytm3 = ytm[:].rearrange("p (hh i) -> p hh i", i=16).transpose((0, 2, 1))
        ytm_toks = []
        for grp in range(2):
            bk = b.bank()
            first = True
            tok_p = None
            for ii in range(8):
                i = grp * 8 + ii
                waits = [scan_done]
                if first and b.bank_cons[bk] is not None:
                    waits.append(b.bank_cons[bk])
                first = False
                tok_p = s.emit("tensor", lambda h, i=i, ii=ii, bk=bk:
                               h.transpose(b.psum[bk][:].rearrange(
                                   "p (n m) -> p n m", m=64)[:, ii, :],
                                   y_sc[:, i, :], b.identf_sb[0:64, 0:64]),
                               after=waits, tag="ytr")
            b.bank_prod[bk] = tok_p
            ytm_toks.append(b.consume(
                "scalar", bk, lambda h, bk=bk, grp=grp:
                h.copy(ytm3[:, grp * 8:grp * 8 + 8, :],
                       b.psum[bk][:].rearrange("p (n m) -> p n m", m=64)),
                tag="ytmev"))
        bk = b.bank()
        pieces = [((lambda ba, c=c: ba.rearrange("p (n m) -> p n m", m=128)[:, c, :]),
                   (lambda c=c: ytm[:, c * 128:(c + 1) * 128]),
                   list(ytm_toks)) for c in range(KC)]
        b.transpose_group(bk, pieces, bf16=True, tag="yftr")
        yfm_toks = [b.consume("scalar", bk, lambda h, bk=bk: h.copy(
            yfmt[:],
            b.psum[bk][:].bitcast(BF16).rearrange("p (n m) -> p n m", m=128)),
            tag="yfmev")]
        xt = xt_sb[t % 2]
        tlx = s.emit("sync", lambda h, t=t, xt=xt: h.dma_start(
            out=xt[:], in_=x3[t]), after=[xt_busy[t % 2]], dma=True, tag="xre")
        half_toks = []
        for half in range(2):
            bk = b.bank()
            pieces = [((lambda k=k, t=t: yfmt[:, k, :]),
                       (lambda k=k, half=half: wout_sb[:, k, half * 512:(half + 1) * 512]),
                       yfm_toks + [tok_wout[k]]) for k in range(KC)]
            b.mm_acc(bk, pieces, tag="woutmm")
            half_toks.append(b.consume(
                "vector", bk, lambda h, bk=bk, half=half, xt=xt:
                h.scalar_tensor_tensor(x1t[:, half * 512:(half + 1) * 512],
                                       xt[:, half * 512:(half + 1) * 512], 2.0,
                                       b.psum[bk][:], op0=ALU.mult, op1=ALU.add),
                after=[tlx] + x1t_busy, tag="resadd"))
        xt_busy[t % 2] = half_toks[-1]
        t_ln = b.spine_ln(x1t[:], x1t[:], half_toks, tag="ln1",
                          scratch_guard=[tmb_busy])
        sp = s.emit("sync", lambda h, t=t: h.dma_start(
            out=x_spill[t], in_=x1t[:]), after=[t_ln], dma=True, tag="x1spill")
        tok_x1spill.append(sp)
        if "x1" in b.dbg_taps:
            spd = s.emit("sync", lambda h, t=t: h.dma_start(
                out=dbg_x1[t], in_=x1t[:]), after=[t_ln], dma=True, tag="dbgx1")
            b.dbg_tokens.append(spd)
            tok_x1spill.append(spd)
        t_cast = s.emit("scalar", lambda h: h.copy(tmb[:], x1t[:]),
                        after=[t_ln], tag="x1cast")
        x1t_busy = [sp, t_cast] + ([spd] if "x1" in b.dbg_taps else [])
        bk = b.bank()
        pieces = [((lambda ba, c=c: ba.rearrange(
            "p (n m) -> p n m", m=128)[:, c, :]),
            (lambda c=c: tmb[:, c * 128:(c + 1) * 128]),
            [t_cast]) for c in range(KC)]
        tmb_busy = b.transpose_group(bk, pieces, bf16=True, tag="x1tr")
        evt = b.consume("scalar", bk, lambda h, bk=bk: h.copy(
            x1fmt[:],
            b.psum[bk][:].bitcast(BF16).rearrange("p (n m) -> p n m", m=128)),
            after=[tok_xfm[t - 1] if t else None], tag="x1evac")
        tok_xfm[t] = s.emit("sync", lambda h, t=t: h.dma_start(
            out=node_spill[6, :, :, t * 128:(t + 1) * 128].transpose((1, 0, 2)) if False
            else node_spill[6].transpose((1, 0, 2))[:, :, t * 128:(t + 1) * 128],
            in_=x1fmt[:]), after=[evt], dma=True, tag="x1fmsp")
    ctx["tok_xfm_p1"] = tok_xfm
    ctx["tok_x1spill"] = tok_x1spill
    b.close_phase()
    ctx["p1_done"] = True


def zmap():
    m = {}
    for h in range(4):
        ie, io = 0, 0
        for p in range(32):
            q = p // 2
            if q % 2 == 0:
                m[(h, p)] = (h * 16 + ie, 0)
                ie += 1
            else:
                m[(h, p)] = (h * 16 + io, 64)
                io += 1
    return m


ZMAP = zmap()


EDGES = [
    (0, 0, 0, 6, 6, 6),
    (2, 1, 2, 0, 6, 0),
    (5, 3, 4, 1, 6, 0),
    (9, 6, 8, 2, 6, 1),
    (14, 10, 12, 3, 6, 1),
    (19, 15, 17, 4, 0, 2),
]
RSQ = float(1.0 / np.sqrt(128.0))


def p2(ctx):
    b = ctx["b"]; s = ctx["s"]; g = ctx["g"]
    stat = ctx["stat"]; tmb = ctx["tmb"]
    xt_sb = ctx["xt_sb"]
    node_spill = ctx["node_spill"]; q_spill = ctx["q_spill"]
    x_spill = ctx["x_spill"]
    LN_EPS = 1e-6

    b.open_phase()
    qh = b.sbuf("qh", [128, KC, T], BF16, phase=True)
    kh = b.sbuf("kh", [128, KC, T], BF16, phase=True)
    n5tm_a = kh[:, 0, 0:2048].bitcast(F32)

    vh = b.sbuf("vh", [128, NT, D], BF16, phase=True)
    Ofm = b.sbuf("Ofm", [128, KC, T], BF16, phase=True)
    wpool = [b.sbuf(f"wp{i}", [128, KC, D], BF16, phase=True) for i in range(2)]
    wbusy = [None, None]
    wtok = [None, None]
    wkey = [None, None]
    jit = [b.sbuf(f"jit{i}", [128, 512], BF16, phase=True) for i in range(10)]
    jit_busy = [None] * 10
    Pbuf = b.sbuf("Pbuf", [128, T], BF16, phase=True)
    q5tf = Pbuf[:].bitcast(F32)
    PT = b.sbuf("PT", [128, NT, 128], BF16, phase=True)
    nmax2 = b.sbuf("nmax2", [128, 8], F32, phase=True)
    ssum = b.sbuf("ssum", [128, 8], F32, phase=True)
    rcol = b.sbuf("rcol", [128, 1], F32, phase=True)
    bec = b.sbuf("bec", [128, 8], F32, phase=True)
    ew_sb = b.sbuf("ew_sb2", [128, 18], F32, phase=True)
    actw_sb = b.sbuf("actw_sb2", [128, 6], F32, phase=True)
    lng_sb = b.sbuf("lng_sb2", [128, D], BF16, phase=True)
    lnb_sb = b.sbuf("lnb_sb2", [128, D], BF16, phase=True)
    ncomb = [b.sbuf(f"ncomb{i}", [128, 512], BF16, phase=True) for i in range(2)]

    qstage_t = b.sbuf("qstage", [128, 4, 512], F32, phase=True)
    qstage = qstage_t[:, 0, :]
    qjit = [b.sbuf(f"qjit{i}", [128, 512], F32, phase=True) for i in range(2)]
    ncomb_busy = [None, None]
    n5tm = n5tm_a
    x2t = n5tm_a

    ctok2 = []
    for dst, src in ((ew_sb, "ew_rep"), (actw_sb, "actw_rep")):
        ctok2.append(s.emit("sync", lambda h, d=dst, sr=src: h.dma_start(
            out=d[:], in_=g[sr]), dma=True, tag="c2"))
    for dst, src in ((lng_sb, "ln_g_rep"), (lnb_sb, "ln_b_rep")):
        ctok2.append(s.emit("gpsimd", lambda h, d=dst, sr=src: h.dma_start(
            out=d[:], in_=g[sr]), dma=True, tag="c2"))
    ct2 = ctok2[-1]

    slot_done = {}

    jit_rr = [0]
    bec_busy = [None]
    qstage_busy = [None, None, None, None]
    qstage_rr = [0]

    def jit_load(src_ap, guard, after=(), tag="jit", width=512):
        bi = jit_rr[0]
        jit_rr[0] = (jit_rr[0] + 1) % 10
        tok = s.emit("sync", lambda h, src_ap=src_ap, bi=bi, width=width:
                     h.dma_start(out=jit[bi][:, 0:width], in_=src_ap),
            after=[jit_busy[bi], guard] + list(after), dma=True, tag=tag)
        return bi, tok

    wrr = [0]

    def wload_pool(src_ap, key, tag="wp"):
        for i in range(2):
            if wkey[i] == key:
                return i, wtok[i]
        i = wrr[0]
        wrr[0] = (wrr[0] + 1) % 2
        toks = []
        for k in range(KC):
            toks.append(s.emit("gpsimd", lambda h, i=i, k=k, src_ap=src_ap:
                               h.dma_start(out=wpool[i][:, k, :],
                                           in_=src_ap[k * 128:(k + 1) * 128, :]),
                               after=[wbusy[i]], dma=True, tag=tag))
        wkey[i] = key
        wtok[i] = toks[-1]
        return i, wtok[i]

    def edge_sweep(e_w, key, src_slot, j_scale, out_fn, tqs, tq0=0,
                   extra=(), tag="edge", bias=True, ev_extra=(), f32_spill=False):
        wi, wt = wload_pool(g["We"][e_w], ("We", e_w), tag=tag + "w")
        tl = s.emit("sync", lambda h: h.dma_start(
            out=bec[:], in_=g["be_cols"][e_w]), after=list(extra) + [bec_busy[0]],
            dma=True, tag=tag + "becl")
        tb = s.emit("vector", lambda h: h.tensor_scalar(
            bec[:], bec[:], ew_sb[:, j_scale:j_scale + 1], None, op0=ALU.mult),
            after=[tl, ct2], tag=tag + "becm")
        tb = s.emit("vector", lambda h: h.drain(), after=[tb], tag=tag + "becd")
        ev_toks = []
        for tq in range(tq0, tq0 + tqs):
            bufs = [jit_load(node_spill[src_slot, k, :, tq * 512:(tq + 1) * 512],
                             slot_done.get(src_slot), tag=tag + "jit")
                    for k in range(KC)]
            last_mm = None
            for oc in range(8):
                pieces = []
                for k in range(KC):
                    bi, ltok = bufs[k]
                    pieces.append((
                        (lambda k=k, oc=oc, wi=wi: wpool[wi][:, k, oc * 128:(oc + 1) * 128]),
                        (lambda bi=bi: jit[bi][:]), [wt, ltok, tb]))
                b.mm_acc(oc, pieces, tag=tag)
                last_mm = b.bank_prod[oc]
            for bi, _ in bufs:
                jit_busy[bi] = last_mm
            for oc in range(8):
                ev_toks.append(b.consume("scalar", oc,
                               lambda h, oc=oc, tq=tq: h.activation(
                                   out_fn(oc, tq), b.psum[oc][:], AF.Identity,
                                   bias=bec[:, oc:oc + 1],
                                   scale=ew_sb[:, j_scale:j_scale + 1]),
                               after=(list(ev_extra) if (tq == tq0 and oc == 0) else []),
                               tag=tag + "ev"))
                if f32_spill:
                    qi = qstage_rr[0]
                    qstage_rr[0] = (qi + 1) % 4
                    evf = b.consume("scalar", oc,
                                    lambda h, oc=oc, qi=qi: h.activation(
                                        qstage_t[:, qi, :], b.psum[oc][:], AF.Identity,
                                        bias=bec[:, oc:oc + 1],
                                        scale=ew_sb[:, j_scale:j_scale + 1]),
                                    after=[qstage_busy[qi]], tag=tag + "evf")
                    spq = s.emit("sync", lambda h, oc=oc, tq=tq, qi=qi: h.dma_start(
                        out=q_spill[oc, :, tq * 512:(tq + 1) * 512],
                        in_=qstage_t[:, qi, :]),
                        after=[evf, ctx.get("q_read_done")], dma=True,
                        tag=tag + "qspf")
                    qstage_busy[qi] = spq
                    ctx["_qspill_tok"] = spq
        wbusy[wi] = ev_toks[-1]
        bec_busy[0] = ev_toks[-1]
        return ev_toks

    def proj_sweep(w_name, w_idx, src_fn, out_fn, tqs, src_toks, tag="proj"):
        wi, wt = wload_pool(g[w_name][w_idx], (w_name, w_idx), tag=tag + "w")
        ev_toks = []
        last_mm = None
        for tq in range(tqs):
            for oc in range(8):
                pieces = [((lambda k=k, oc=oc, wi=wi: wpool[wi][:, k, oc * 128:(oc + 1) * 128]),
                           (lambda k=k, tq=tq: src_fn(k, tq)),
                           [wt] + list(src_toks)) for k in range(KC)]
                b.mm_acc(oc, pieces, tag=tag)
                last_mm = b.bank_prod[oc]
            for oc in range(8):
                ev_toks.append(b.consume("scalar", oc,
                               lambda h, oc=oc, tq=tq: h.copy(
                                   out_fn(oc, tq), b.psum[oc][:]), tag=tag + "ev"))
        wbusy[wi] = last_mm
        return ev_toks

    prev_pt = None
    prev_comb = None
    for c in range(ctx.get("p2_nodes", 6)):
        eq, ek, ev, sq, sk, sv = EDGES[c]
        awc = actw_sb[:, c:c + 1]

        q_evs = edge_sweep(eq, None, sq, c * 3 + 0,
                           lambda oc, tq: Ofm[:, oc, tq * 512:(tq + 1) * 512],
                           4, extra=[prev_comb], tag=f"qe{c}", f32_spill=True)
        qspt = ctx.pop("_qspill_tok")
        qh_evs = proj_sweep("Wq0", c,
                            lambda k, tq: Ofm[:, k, tq * 512:(tq + 1) * 512],
                            lambda oc, tq: qh[:, oc, tq * 512:(tq + 1) * 512],
                            4, q_evs, tag=f"qh{c}")

        kh_evs = []
        for qq in range(4):
            ke = edge_sweep(ek, None, sk, c * 3 + 1,
                            lambda oc, tq: Ofm[:, oc, 0:512],
                            1, tq0=qq, extra=qh_evs, tag=f"ke{c}_{qq}",
                            ev_extra=([qspt] if qq == 0 else []))
            kh_evs += proj_sweep("Wk0", c,
                                 lambda k, tq: Ofm[:, k, 0:512],
                                 lambda oc, tq, qq=qq: kh[:, oc, qq * 512:(qq + 1) * 512],
                                 1, ke, tag=f"kh{c}_{qq}")

        vh_evs = []
        for qq in range(4):
            ve = edge_sweep(ev, None, sv, c * 3 + 2,
                            lambda oc, tq: Ofm[:, oc, 0:512],
                            1, tq0=qq, extra=kh_evs, tag=f"ve{c}_{qq}")
            wiv, wtv = wload_pool(g["Wv0"][c], ("Wv0", c), tag=f"vh{c}w")
            last_mm = None
            for tt in range(4):
                gt = qq * 4 + tt
                for dv in range(2):
                    bk = (tt * 2 + dv) % 8
                    pieces = [((lambda k=k, tt=tt: Ofm[:, k, tt * 128:(tt + 1) * 128]),
                               (lambda k=k, dv=dv, wiv=wiv: wpool[wiv][:, k, dv * 512:(dv + 1) * 512]),
                               [wtv] + ve) for k in range(KC)]
                    b.mm_acc(bk, pieces, tag=f"vh{c}")
                    last_mm = b.bank_prod[bk]
                    vh_evs.append(b.consume("scalar", bk,
                                   lambda h, bk=bk, gt=gt, dv=dv: h.copy(
                                       vh[:, gt, dv * 512:(dv + 1) * 512],
                                       b.psum[bk][:]), tag=f"vh{c}ev"))
            wbusy[wiv] = last_mm

        o_evs = []
        for hh in range(8):
            for qt in range(NT):
                sb = []
                for kc in range(4):
                    bk = kc
                    pieces = [((lambda hh=hh, qt=qt: qh[:, hh, qt * 128:(qt + 1) * 128]),
                               (lambda hh=hh, kc=kc: kh[:, hh, kc * 512:(kc + 1) * 512]),
                               qh_evs + kh_evs)]
                    b.mm_acc(bk, pieces, tag=f"qk{c}")
                    sb.append(b.consume("vector", bk,
                              lambda h, bk=bk, kc=kc: h.reduce_max(
                                  nmax2[:, kc:kc + 1], b.psum[bk][:], axis=AX.X,
                                  negate=True), tag=f"qkm{c}"))
                t_d = s.emit("vector", lambda h: h.drain(), after=sb, tag="qkmd")
                t_mx = s.emit("vector", lambda h: h.tensor_reduce(
                    nmax2[:, 4:5], nmax2[:, 0:4], axis=AX.X, op=ALU.min),
                    after=[t_d], tag="qkm2")
                t_mx = s.emit("vector", lambda h: h.drain(), after=[t_mx], tag="qkm2d")
                t_mx = s.emit("vector", lambda h: h.tensor_scalar(
                    nmax2[:, 5:6], nmax2[:, 4:5], RSQ, None, op0=ALU.mult),
                    after=[t_mx], tag="qkm3")
                t_mx = s.emit("vector", lambda h: h.drain(), after=[t_mx], tag="qkm3d")
                ex_toks = []
                for kc in range(4):
                    aft = [t_mx] + ([prev_pt] if kc == 0 else [])
                    ex_toks.append(b.consume("scalar", kc,
                                   lambda h, kc=kc: h.activation(
                                       Pbuf[:, kc * 512:(kc + 1) * 512],
                                       b.psum[kc][:], AF.Exp,
                                       bias=nmax2[:, 5:6], scale=RSQ,
                                       accum_out=ssum[:, kc:kc + 1]),
                                   after=aft, tag=f"ex{c}"))
                t_sm = s.emit("vector", lambda h: h.reduce_sum(
                    ssum[:, 4:5], ssum[:, 0:4], axis=AX.X), after=ex_toks,
                    tag="psm")
                t_sm = s.emit("vector", lambda h: h.drain(), after=[t_sm], tag="psmd")
                t_rc = s.emit("vector", lambda h: h.reciprocal(
                    rcol[:], ssum[:, 4:5]), after=[t_sm], tag="prc")
                t_rc = s.emit("vector", lambda h: h.drain(), after=[t_rc], tag="prcd")
                t_pn = s.emit("vector", lambda h: h.tensor_scalar(
                    Pbuf[:], Pbuf[:], rcol[:], None, op0=ALU.mult),
                    after=[t_rc] + ex_toks, tag="pn")
                pt_evs = []
                for grp in range(2):
                    bkp = 4 + grp
                    pieces = [((lambda ba, jj=jj: ba.rearrange("p (n m) -> p n m", m=128)[:, jj, :]),
                               (lambda kcc=grp * 8 + jj: Pbuf[:, kcc * 128:(kcc + 1) * 128]),
                               [t_pn]) for jj in range(8)]
                    b.transpose_group(bkp, pieces, bf16=True, tag=f"pt{c}")
                    pt_evs.append(b.consume("scalar", bkp,
                                   lambda h, bkp=bkp, grp=grp: h.copy(
                                       PT[:, grp * 8:(grp + 1) * 8, :],
                                       b.psum[bkp][:].bitcast(BF16).rearrange(
                                           "p (n m) -> p n m", m=128)),
                                   tag=f"ptev{c}"))
                prev_pt = b.bank_prod[5]
                bko = 6 if (hh * NT + qt) % 2 == 0 else 7
                pieces = [((lambda kc=kc, hh=hh: vh[:, kc, hh * 128:(hh + 1) * 128]),
                           (lambda kc=kc: PT[:, kc, :]),
                           pt_evs + vh_evs) for kc in range(NT)]
                b.mm_acc(bko, pieces, out_ap=b.psum[bko][:, 0:128], tag=f"pv{c}")
                o_evs.append(b.consume("scalar", bko,
                               lambda h, bko=bko, hh=hh, qt=qt: h.copy(
                                   Ofm[:, hh, qt * 128:(qt + 1) * 128],
                                   b.psum[bko][:, 0:128]),
                               after=([qspt] if (hh == 0 and qt == 0) else []),
                               tag=f"oev{c}"))

        wio, wto = wload_pool(g["Wo0"][c], ("Wo0", c), tag=f"wo{c}w")
        if c < 5:
            comb_toks = []
            qjit_busy = [None, None]
            if ctx.get("p2_bar"):
                s.barrier()
            for tq in range(4):
                for oc in range(8):
                    bi = (tq * 8 + oc) % 2
                    qtok = s.emit("sync", lambda h, oc=oc, tq=tq, bi=bi: h.dma_start(
                        out=qjit[bi][:], in_=q_spill[oc, :, tq * 512:(tq + 1) * 512]),
                        after=[qjit_busy[bi], qspt], dma=True, tag=f"qre{c}")
                    ctx["q_read_done"] = qtok
                    pieces = [((lambda k=k, oc=oc, wio=wio: wpool[wio][:, k, oc * 128:(oc + 1) * 128]),
                               (lambda k=k, tq=tq: Ofm[:, k, tq * 512:(tq + 1) * 512]),
                               [wto] + o_evs) for k in range(KC)]
                    b.mm_acc(oc, pieces, tag=f"wo{c}")
                    nb = (tq * 8 + oc) % 2
                    tcomb = b.consume("vector", oc,
                                      lambda h, oc=oc, bi=bi, nb=nb, awc=awc:
                                      h.scalar_tensor_tensor(
                                          ncomb[nb][:], b.psum[oc][:], awc,
                                          qjit[bi][:], op0=ALU.mult, op1=ALU.add),
                                      after=[qtok, ncomb_busy[nb]], tag=f"cb{c}")
                    sp = s.emit("sync", lambda h, c=c, oc=oc, tq=tq, nb=nb:
                                h.dma_start(
                                    out=node_spill[c, oc, :, tq * 512:(tq + 1) * 512],
                                    in_=ncomb[nb][:]), after=[tcomb], dma=True,
                                tag=f"nsp{c}")
                    qjit_busy[bi] = tcomb
                    ncomb_busy[nb] = sp
                    comb_toks.append(sp)
                    slot_done[c] = sp
                if ctx.get("p2_bar"):
                    s.barrier()
            wbusy[wio] = comb_toks[-1]
            prev_comb = comb_toks[-1]
            if ctx.get("p2_bar"):
                s.barrier()
        else:
            tok_xfm2 = [None] * NT
            for tile in range(NT):
                wo_toks = []
                for dvh in range(2):
                    bk = dvh
                    pieces = [((lambda k=k, tile=tile: Ofm[:, k, tile * 128:(tile + 1) * 128]),
                               (lambda k=k, dvh=dvh, wio=wio: wpool[wio][:, k, dvh * 512:(dvh + 1) * 512]),
                               [wto] + o_evs) for k in range(KC)]
                    b.mm_acc(bk, pieces, tag="wo5")
                    wo_toks.append(b.bank_prod[bk])
                qjits = []
                for ch in range(KC):
                    bi = ch % 2
                    tok = s.emit("sync", lambda h, ch=ch, tile=tile, bi=bi:
                                 h.dma_start(out=qjit[bi][:, 0:128],
                                             in_=q_spill[ch, :, tile * 128:(tile + 1) * 128]),
                                 after=[ctx.get("p2_q5jb%d" % bi), qspt],
                                 dma=True, tag="q5j")
                    ctx["q_read_done"] = tok
                    qjits.append((bi, tok))
                    bkq = 2 + (ch % 2) * 1
                    bkq = 2 if ch < 4 else 4
                    waits = [tok]
                    if ch % 4 == 0 and b.bank_cons[bkq] is not None:
                        waits.append(b.bank_cons[bkq])
                    tok_p = s.emit("tensor", lambda h, bi=bi, ch=ch, bkq=bkq:
                                   h.transpose(b.psum[bkq][:].rearrange(
                                       "p (n m) -> p n m", m=128)[:, ch % 4, :],
                                       qjit[bi][:, 0:128], b.identf()),
                                   after=waits, tag="q5t")
                    ctx["p2_q5jb%d" % bi] = tok_p
                    b.bank_prod[bkq] = tok_p
                    if ch % 4 == 3:
                        b.consume("scalar", bkq, lambda h, bkq=bkq, ch=ch: h.copy(
                            q5tf[:, (ch - 3) * 128:(ch + 1) * 128],
                            b.psum[bkq][:]),
                            after=([prev_pt] if ch == 3 else []), tag="q5ev")
                t_q5 = s.last("scalar")
                cb_toks = []
                for dvh in range(2):
                    cb_toks.append(b.consume("vector", dvh,
                                   lambda h, dvh=dvh, awc=awc: h.scalar_tensor_tensor(
                                       n5tm[:, dvh * 512:(dvh + 1) * 512],
                                       b.psum[dvh][:], awc,
                                       q5tf[:, dvh * 512:(dvh + 1) * 512],
                                       op0=ALU.mult, op1=ALU.add),
                                   after=[t_q5], tag="n5c"))
                t_ln1 = b.spine_ln(n5tm[:], n5tm[:], cb_toks, eps=LN_EPS,
                                   tag=f"iln{tile}", scratch_guard=[])
                t_gb = s.emit("vector", lambda h: h.tensor_tensor(
                    n5tm[:], n5tm[:], lng_sb[:], op=ALU.mult),
                    after=[t_ln1, ct2], tag="gbm")
                t_gb = s.emit("vector", lambda h: h.tensor_tensor(
                    n5tm[:], n5tm[:], lnb_sb[:], op=ALU.add),
                    after=[t_gb], tag="gba")
                xt = xt_sb[tile % 2]
                tlx = s.emit("sync", lambda h, tile=tile, xt=xt: h.dma_start(
                    out=xt[:], in_=x_spill[tile]), after=[ctx.get("p2_xt_busy")],
                    dma=True, tag="x1re")
                t_add = s.emit("vector", lambda h, xt=xt: h.tensor_tensor(
                    x2t[:], n5tm[:], xt[:], op=ALU.add),
                    after=[t_gb, tlx] + ctx.get("p2_x2t_busy", []), tag="x2add")
                ctx["p2_xt_busy"] = t_add
                t_ln2 = b.spine_ln(x2t[:], x2t[:], [t_add], tag=f"ln2_{tile}",
                                   scratch_guard=[])
                spx = s.emit("sync", lambda h, tile=tile: h.dma_start(
                    out=x_spill[tile], in_=x2t[:]), after=[t_ln2], dma=True,
                    tag="x2sp")
                t_cast = s.emit("scalar", lambda h: h.copy(tmb[:], x2t[:]),
                                after=[t_ln2], tag="x2cast")
                ctx["p2_x2t_busy"] = [spx, t_cast]
                bk2 = 3
                pieces = [((lambda ba, ch=ch: ba.rearrange("p (n m) -> p n m", m=128)[:, ch, :]),
                           (lambda ch=ch: tmb[:, ch * 128:(ch + 1) * 128]),
                           [t_cast]) for ch in range(KC)]
                ctx["p2_tmb_busy"] = b.transpose_group(bk2, pieces, bf16=True, tag="x2tr")
                ev2 = b.consume("scalar", bk2,
                                lambda h, bk2=bk2: h.copy(
                                    qstage[:].bitcast(BF16).rearrange(
                                        "p (n m) -> p n m", m=128),
                                    b.psum[bk2][:].bitcast(BF16).rearrange(
                                        "p (n m) -> p n m", m=128)),
                                after=[tok_xfm2[tile - 1] if tile else None,
                                       qstage_busy[0]],
                                tag="x2ev")
                tok_xfm2[tile] = s.emit("sync", lambda h, tile=tile: h.dma_start(
                    out=node_spill[5].transpose((1, 0, 2))[:, :, tile * 128:(tile + 1) * 128],
                    in_=qstage[:].bitcast(BF16).rearrange(
                        "p (n m) -> p n m", m=128)), after=[ev2], dma=True,
                    tag="x2fmsp")
                qstage_busy[0] = tok_xfm2[tile]
                ctx.setdefault("tok_x2spill", []).append(spx)
            ctx["tok_x2fm"] = tok_xfm2
            slot_done[5] = tok_xfm2[-1]
    b.close_phase()
    ctx["p2_done"] = True


def p3(ctx):
    b = ctx["b"]; s = ctx["s"]; g = ctx["g"]
    stat = ctx["stat"]; tmb = ctx["tmb"]; xt_sb = ctx["xt_sb"]
    node_spill = ctx["node_spill"]; x_spill = ctx["x_spill"]
    RSQ64 = float(1.0 / np.sqrt(64.0))

    b.open_phase()
    qh1 = b.sbuf("qh1", [128, KC, T], BF16, phase=True)
    kh1 = b.sbuf("kh1", [128, KC, T], BF16, phase=True)
    vh1 = b.sbuf("vh1", [128, NT, D], BF16, phase=True)
    Ofm1 = b.sbuf("Ofm1", [128, KC, T], BF16, phase=True)
    wpool = [b.sbuf(f"wp3_{i}", [128, KC, D], BF16, phase=True) for i in range(2)]
    wbusy = [None, None]
    jit = [b.sbuf(f"jit3_{i}", [128, 512], BF16, phase=True) for i in range(10)]
    jit_busy = [None] * 10
    jit_rr = [0]
    Pbuf = b.sbuf("Pbuf3", [128, T], BF16, phase=True)
    PT = b.sbuf("PT3", [128, NT, 128], BF16, phase=True)
    nmax2 = b.sbuf("nmax3", [128, 8], F32, phase=True)
    ssum = b.sbuf("ssum3", [128, 8], F32, phase=True)
    rcol = b.sbuf("rcol3", [128, 1], F32, phase=True)
    bcol = b.sbuf("bcol3", [128, 8], F32, phase=True)
    bo1_sb = b.sbuf("bo1_sb3", [128, D], F32, phase=True)
    mask_sb = b.sbuf("mask_sb3", [128, 128], F32, phase=True)
    x3r = b.sbuf("x3r", [128, D], F32, phase=True)
    x3fmt = b.sbuf("x3fmt", [128, KC, 128], BF16, phase=True)

    c1 = s.emit("sync", lambda h: h.dma_start(out=bo1_sb[:], in_=g["bo1_rep"]),
                dma=True, tag="bo1l")
    c2 = s.emit("sync", lambda h: h.dma_start(out=mask_sb[:], in_=g["mask128"]),
                dma=True, tag="maskl")

    def jload(src_ap, after=(), tag="j3", width=512):
        bi = jit_rr[0]
        jit_rr[0] = (bi + 1) % 10
        tok = s.emit("sync", lambda h, src_ap=src_ap, bi=bi, width=width:
                     h.dma_start(out=jit[bi][:, 0:width], in_=src_ap),
                     after=[jit_busy[bi]] + list(after), dma=True, tag=tag)
        return bi, tok

    def wload(name, slot, after=(), tag="w3"):
        toks = []
        for k in range(KC):
            toks.append(s.emit("gpsimd", lambda h, k=k, slot=slot, name=name:
                               h.dma_start(out=wpool[slot][:, k, :],
                                           in_=g[name][k * 128:(k + 1) * 128, :]),
                               after=[wbusy[slot]] + list(after), dma=True,
                               tag=tag))
        return toks[-1]

    def proj1(wname, bname, dst, slot, tag):
        wt = wload(wname, slot, tag=tag + "w")
        bl = s.emit("sync", lambda h: h.dma_start(
            out=bcol[:], in_=g[bname]), after=[ctx.get("p3_bcol_busy")],
            dma=True, tag=tag + "b")
        evs = []
        last_mm = None
        for tq in range(4):
            bufs = [jload(node_spill[5, k, :, tq * 512:(tq + 1) * 512],
                          tag=tag + "j") for k in range(KC)]
            for oc in range(8):
                pieces = []
                for k in range(KC):
                    bi, ltok = bufs[k]
                    pieces.append((
                        (lambda k=k, oc=oc, slot=slot: wpool[slot][:, k, oc * 128:(oc + 1) * 128]),
                        (lambda bi=bi: jit[bi][:]), [wt, ltok, bl]))
                b.mm_acc(oc, pieces, tag=tag)
                last_mm = b.bank_prod[oc]
            for bi, _ in bufs:
                jit_busy[bi] = last_mm
            for oc in range(8):
                evs.append(b.consume("scalar", oc,
                           lambda h, oc=oc, tq=tq: h.activation(
                               dst(oc, tq), b.psum[oc][:], AF.Identity,
                               bias=bcol[:, oc:oc + 1], scale=1.0),
                           tag=tag + "ev"))
        wbusy[slot] = last_mm
        ctx["p3_bcol_busy"] = evs[-1]
        return evs

    qh_evs = proj1("Wq1", "bq1", lambda oc, tq: qh1[:, oc, tq * 512:(tq + 1) * 512],
                   0, "q1")
    kh_evs = proj1("Wk1", "bk1", lambda oc, tq: kh1[:, oc, tq * 512:(tq + 1) * 512],
                   1, "k1")
    vf_evs = proj1("Wv1", "bv1", lambda oc, tq: Ofm1[:, oc, tq * 512:(tq + 1) * 512],
                   0, "v1")
    vh_evs = []
    for t in range(NT):
        bk = b.bank()
        pieces = [((lambda ba, c=c: ba.rearrange("p (n m) -> p n m", m=128)[:, c, :]),
                   (lambda c=c, t=t: Ofm1[:, c, t * 128:(t + 1) * 128]),
                   vf_evs) for c in range(KC)]
        b.transpose_group(bk, pieces, bf16=True, tag="vtr")
        vh_evs.append(b.consume("scalar", bk, lambda h, bk=bk, t=t: h.copy(
            vh1[:, t, :], b.psum[bk][:].bitcast(BF16)), tag="vtrev"))

    prev_pt = None
    o_evs = []
    for hh in range(16):
        ch, base = hh // 2, (hh % 2) * 64
        for qt in range(NT):
            nfull = (qt * 128) // 512
            rem = qt * 128 - nfull * 512
            valid = qt * 128 + 128
            used = []
            for kc in range(nfull):
                bk = kc % 3
                pieces = [((lambda ch=ch, base=base, qt=qt:
                            qh1[base:base + 64, ch, qt * 128:(qt + 1) * 128]),
                           (lambda ch=ch, base=base, kc=kc:
                            kh1[base:base + 64, ch, kc * 512:(kc + 1) * 512]),
                           qh_evs + kh_evs)]
                b.mm_acc(bk, pieces, tag="qk1")
                used.append((bk, kc * 512, 512))
            if rem:
                bk = 3
                pieces = [((lambda ch=ch, base=base, qt=qt:
                            qh1[base:base + 64, ch, qt * 128:(qt + 1) * 128]),
                           (lambda ch=ch, base=base, nfull=nfull, rem=rem:
                            kh1[base:base + 64, ch, nfull * 512:nfull * 512 + rem]),
                           qh_evs + kh_evs)]
                b.mm_acc(bk, pieces, out_ap=b.psum[3][:, 0:rem], tag="qk1r")
                used.append((3, nfull * 512, rem))
            bkd = 4
            pieces = [((lambda ch=ch, base=base, qt=qt:
                        qh1[base:base + 64, ch, qt * 128:(qt + 1) * 128]),
                       (lambda ch=ch, base=base, qt=qt:
                        kh1[base:base + 64, ch, qt * 128:(qt + 1) * 128]),
                       qh_evs + kh_evs)]
            b.mm_acc(bkd, pieces, out_ap=b.psum[4][:, 0:128], tag="qk1d")
            t_msk = b.consume("vector", bkd, lambda h: h.scalar_tensor_tensor(
                b.psum[4][:, 0:128], mask_sb[:], float(np.sqrt(64.0)),
                b.psum[4][:, 0:128], op0=ALU.mult, op1=ALU.add),
                after=[c2], tag="mask1")
            used.append((4, qt * 128, 128))
            sb = []
            for i, (bk, off, wid) in enumerate(used):
                aft = [t_msk] if bk == 4 else []
                sb.append(b.consume("vector", bk, lambda h, bk=bk, wid=wid, i=i:
                          h.reduce_max(nmax2[:, i:i + 1], b.psum[bk][:, 0:wid],
                                       axis=AX.X, negate=True),
                          after=aft, tag="qk1m"))
            nb = len(used)
            t_d = s.emit("vector", lambda h: h.drain(), after=sb, tag="m1d")
            t_mx = s.emit("vector", lambda h, nb=nb: h.tensor_reduce(
                nmax2[:, 6:7], nmax2[:, 0:nb], axis=AX.X, op=ALU.min),
                after=[t_d], tag="m1")
            t_mx = s.emit("vector", lambda h: h.drain(), after=[t_mx], tag="m1dd")
            t_mx = s.emit("vector", lambda h: h.tensor_scalar(
                nmax2[:, 7:8], nmax2[:, 6:7], RSQ64, None, op0=ALU.mult),
                after=[t_mx], tag="m1s")
            t_mx = s.emit("vector", lambda h: h.drain(), after=[t_mx], tag="m1sd")
            ex_toks = []
            for i, (bk, off, wid) in enumerate(used):
                aft = [t_mx] + ([prev_pt] if i == 0 else [])
                ex_toks.append(b.consume("scalar", bk,
                               lambda h, bk=bk, off=off, wid=wid, i=i: h.activation(
                                   Pbuf[:, off:off + wid], b.psum[bk][:, 0:wid],
                                   AF.Exp, bias=nmax2[:, 7:8], scale=RSQ64,
                                   accum_out=ssum[:, i:i + 1]),
                               after=aft, tag="ex1"))
            t_sm = s.emit("vector", lambda h, nb=nb: h.reduce_sum(
                ssum[:, 6:7], ssum[:, 0:nb], axis=AX.X), after=ex_toks, tag="ps1")
            t_sm = s.emit("vector", lambda h: h.drain(), after=[t_sm], tag="ps1d")
            t_rc = s.emit("vector", lambda h: h.reciprocal(
                rcol[:], ssum[:, 6:7]), after=[t_sm], tag="rc1")
            t_rc = s.emit("vector", lambda h: h.drain(), after=[t_rc], tag="rc1d")
            t_pn = s.emit("vector", lambda h, valid=valid: h.tensor_scalar(
                Pbuf[:, 0:valid], Pbuf[:, 0:valid], rcol[:], None, op0=ALU.mult),
                after=[t_rc] + ex_toks, tag="pn1")
            ntile = qt + 1
            pt_evs = []
            for grp in range((ntile + 7) // 8):
                bkp = 5 + grp
                cnt = min(8, ntile - grp * 8)
                pieces = [((lambda ba, jj=jj: ba.rearrange(
                            "p (n m) -> p n m", m=128)[:, jj, :]),
                           (lambda kcc=grp * 8 + jj: Pbuf[:, kcc * 128:(kcc + 1) * 128]),
                           [t_pn]) for jj in range(cnt)]
                b.transpose_group(bkp, pieces, bf16=True, tag="pt1")
                pt_evs.append(b.consume("scalar", bkp,
                               lambda h, bkp=bkp, grp=grp, cnt=cnt: h.copy(
                                   PT[:, grp * 8:grp * 8 + cnt, :],
                                   b.psum[bkp][:].bitcast(BF16).rearrange(
                                       "p (n m) -> p n m", m=128)[:, 0:cnt, :]),
                               tag="pt1ev"))
                prev_pt = b.bank_prod[bkp]
            bko = 7
            pieces = [((lambda kc=kc, hh=hh: vh1[:, kc, hh * 64:(hh + 1) * 64]),
                       (lambda kc=kc: PT[:, kc, :]),
                       pt_evs + vh_evs) for kc in range(ntile)]
            b.mm_acc(bko, pieces, out_ap=b.psum[bko][base:base + 64, 0:128],
                     tag="pv1")
            o_evs.append(b.consume("scalar", bko,
                           lambda h, bko=bko, ch=ch, base=base, qt=qt: h.copy(
                               Ofm1[base:base + 64, ch, qt * 128:(qt + 1) * 128],
                               b.psum[bko][base:base + 64, 0:128]),
                           after=(vh_evs if (hh == 0 and qt == 0) else []),
                           tag="oev1"))

    wt_o = wload("Wo1", 0, tag="wo1w")
    tok_x3fm = [None] * NT
    for tile in range(NT):
        wo_toks = []
        for dv in range(2):
            bk = dv
            pieces = [((lambda k=k, tile=tile: Ofm1[:, k, tile * 128:(tile + 1) * 128]),
                       (lambda k=k, dv=dv: wpool[0][:, k, dv * 512:(dv + 1) * 512]),
                       [wt_o] + o_evs) for k in range(KC)]
            b.mm_acc(bk, pieces, tag="wo1")
            wo_toks.append(b.bank_prod[bk])
        xt = xt_sb[tile % 2]
        tlx = s.emit("sync", lambda h, tile=tile, xt=xt: h.dma_start(
            out=xt[:], in_=x_spill[tile]), after=[ctx.get("p3_xt_busy")],
            dma=True, tag="x2re")
        add_toks = []
        for dv in range(2):
            t1 = b.consume("vector", dv, lambda h, dv=dv: h.scalar_tensor_tensor(
                x3r[:, dv * 512:(dv + 1) * 512], b.psum[dv][:], 1.0,
                bo1_sb[:, dv * 512:(dv + 1) * 512], op0=ALU.mult, op1=ALU.add),
                after=[c1] + ctx.get("p3_x3r_busy", []), tag="bo1a")
            add_toks.append(s.emit("vector", lambda h, dv=dv, xt=xt: h.tensor_tensor(
                x3r[:, dv * 512:(dv + 1) * 512], x3r[:, dv * 512:(dv + 1) * 512],
                xt[:, dv * 512:(dv + 1) * 512], op=ALU.add),
                after=[t1, tlx], tag="x3add"))
        ctx["p3_xt_busy"] = add_toks[-1]
        t_ln = b.spine_ln(x3r[:], x3r[:], add_toks, tag=f"ln3_{tile}")
        spx = s.emit("sync", lambda h, tile=tile: h.dma_start(
            out=x_spill[tile], in_=x3r[:]), after=[t_ln], dma=True, tag="x3sp")
        t_cast = s.emit("scalar", lambda h: h.copy(tmb[:], x3r[:]),
                        after=[t_ln], tag="x3cast")
        ctx["p3_x3r_busy"] = [spx, t_cast]
        bk2 = 3
        pieces = [((lambda ba, chn=chn: ba.rearrange("p (n m) -> p n m", m=128)[:, chn, :]),
                   (lambda chn=chn: tmb[:, chn * 128:(chn + 1) * 128]),
                   [t_cast, ctx.get("p3_tmb_busy")]) for chn in range(KC)]
        ctx["p3_tmb_busy"] = b.transpose_group(bk2, pieces, bf16=True, tag="x3tr")
        ev2 = b.consume("scalar", bk2, lambda h, bk2=bk2: h.copy(
            x3fmt[:], b.psum[bk2][:].bitcast(BF16).rearrange(
                "p (n m) -> p n m", m=128)),
            after=[tok_x3fm[tile - 1] if tile else None], tag="x3ev")
        tok_x3fm[tile] = s.emit("sync", lambda h, tile=tile: h.dma_start(
            out=node_spill[4].transpose((1, 0, 2))[:, :, tile * 128:(tile + 1) * 128],
            in_=x3fmt[:]), after=[ev2], dma=True, tag="x3fmsp")
        ctx.setdefault("tok_x3spill", []).append(spx)
    ctx["tok_x3fm"] = tok_x3fm
    if "p3t" in b.dbg_taps:
        for nm, ap_, shp in (("qh1", qh1, (128, KC, T)), ("kh1", kh1, (128, KC, T)),
                             ("vh1", vh1, (128, NT, D)), ("O1", Ofm1, (128, KC, T))):
            o_ = b.dout("dbg_" + nm, shp, BF16)
            b.dbg_tokens.append(s.emit("sync", lambda h, o_=o_, ap_=ap_:
                h.dma_start(out=o_, in_=ap_[:]), after=o_evs + [tok_x3fm[-1]],
                dma=True, tag="dbg"))
    b.close_phase()
    ctx["p3_done"] = True


def p4(ctx):
    b = ctx["b"]; s = ctx["s"]; g = ctx["g"]
    xt_sb = ctx["xt_sb"]; x_spill = ctx["x_spill"]
    node_spill = ctx["node_spill"]; h_spill = ctx["h_spill"]
    y_out = ctx["y_out"]

    b.open_phase()
    wy_sb = b.sbuf("wy_sb", [128, 32, 128], BF16, phase=True)
    wz_sb = b.sbuf("wz_sb", [128, 64, 128], BF16, phase=True)
    ybuf = b.sbuf("ybuf", [128, 32, 128], BF16, phase=True)
    hstage = [b.sbuf(f"hstage{i}", [128, 8, 128], BF16, phase=True) for i in range(2)]
    hstage_busy = [None, None]
    jit4 = [b.sbuf(f"jit4_{i}", [128, 128], BF16, phase=True) for i in range(10)]
    jit4_busy = [None] * 10
    jr = [0]

    tok_wy = []
    for yc in range(32):
        tok_wy.append(s.emit("gpsimd", lambda h, yc=yc: h.dma_start(
            out=wy_sb[:, yc, :], in_=g["Wyp"].rearrange(
                "a b p m -> (a b) p m")[yc]), dma=True, tag="wyl"))
    tok_wz = []
    for t in range(64):
        tok_wz.append(s.emit("gpsimd", lambda h, t=t: h.dma_start(
            out=wz_sb[:, t, :], in_=g["Wzp"][t]), dma=True, tag="wzl"))

    for tile in range(NT):
        xj = []
        for ch in range(KC):
            bi = jr[0]
            jr[0] = (bi + 1) % 10
            tok = s.emit("sync", lambda h, ch=ch, tile=tile, bi=bi: h.dma_start(
                out=jit4[bi][:], in_=node_spill[4, ch, :, tile * 128:(tile + 1) * 128]),
                after=[jit4_busy[bi], ctx["tok_x3fm"][tile]], dma=True, tag="x3j")
            xj.append((bi, tok))
        yev = []
        last_mm = None
        for hh in range(4):
            for rc in range(KC):
                yc = hh * 8 + rc
                bk = yc % 6
                bi, ltok = xj[rc]
                pieces = [((lambda yc=yc: wy_sb[:, yc, :]),
                           (lambda bi=bi: jit4[bi][:]),
                           [tok_wy[yc], ltok])]
                b.mm_acc(bk, pieces, out_ap=b.psum[bk][:, 0:128], tag="ymm")
                last_mm = b.bank_prod[bk]
                yev.append(b.consume("scalar", bk, lambda h, bk=bk, yc=yc: h.copy(
                    ybuf[:, yc, :], b.psum[bk][:, 0:128]), tag="yev"))
        for bi, _ in xj:
            jit4_busy[bi] = last_mm
        for grp in range(16):
            hs = grp % 2
            zev = []
            for gg in range(8):
                zc = grp * 8 + gg
                hh, p = zc // 32, zc % 32
                tz, zbase = ZMAP[(hh, p)]
                q = p // 2
                ybase = (q % 2) * 64
                ychunk = hh * 8 + q // 2
                bk = 6 + (gg % 2)
                pieces = [((lambda tz=tz, zbase=zbase: wz_sb[zbase:zbase + 64, tz, :]),
                           (lambda ybase=ybase, ychunk=ychunk:
                            ybuf[ybase:ybase + 64, ychunk, :]),
                           [tok_wz[tz]] + yev)]
                b.mm_acc(bk, pieces, out_ap=b.psum[bk][:, 0:128], tag="zmm")
                zev.append(b.consume("scalar", bk,
                           lambda h, bk=bk, gg=gg, hs=hs: h.activation(
                               hstage[hs][:, gg, :], b.psum[bk][:, 0:128],
                               AF.Gelu),
                           after=([hstage_busy[hs]] if gg == 0 else []),
                           tag="gelu"))
            sp = s.emit("sync", lambda h, tile=tile, grp=grp, hs=hs: h.dma_start(
                out=h_spill[tile, grp * 8:(grp + 1) * 8].transpose((1, 0, 2)),
                in_=hstage[hs][:]), after=zev, dma=True, tag="hsp")
            hstage_busy[hs] = sp
            ctx.setdefault("tok_hsp", []).append(sp)
    b.close_phase()

    b.open_phase()
    wc_sb = b.sbuf("wc_sb", [128, 128, 512], BF16, phase=True)
    hjit = [b.sbuf(f"hjit{i}", [128, 8, 128], BF16, phase=True) for i in range(3)]
    hjit_busy = [None] * 3
    x4stash = b.sbuf("x4stash", [128, NT, 512], F32, phase=True)
    x4r = b.sbuf("x4r", [128, D], F32, phase=True)
    hsp_done = ctx["tok_hsp"][-1]

    for dh in range(2):
        tok_wc = []
        for zc in range(128):
            tok_wc.append(s.emit("gpsimd", lambda h, zc=zc, dh=dh: h.dma_start(
                out=wc_sb[:, zc, :],
                in_=g["WcT"][zc * 128:(zc + 1) * 128, dh * 512:(dh + 1) * 512]),
                after=[ctx.get("p4_wc_busy")], dma=True, tag="wcl"))
        for tile in range(NT):
            bk = tile % 2
            last_mm = None
            mm_toks = []
            for zg in range(16):
                hb = zg % 3
                htok = s.emit("sync", lambda h, tile=tile, zg=zg, hb=hb: h.dma_start(
                    out=hjit[hb][:], in_=h_spill[tile, zg * 8:(zg + 1) * 8]
                    .transpose((1, 0, 2))), after=[hjit_busy[hb], hsp_done],
                    dma=True, tag="hj")
                for gg in range(8):
                    zc = zg * 8 + gg
                    start = (zc == 0)
                    stop = (zc == 127)
                    waits = [htok, tok_wc[zc]]
                    if start and b.bank_cons[bk] is not None:
                        waits.append(b.bank_cons[bk])
                    last_mm = s.emit("tensor", lambda h, bk=bk, hb=hb, gg=gg,
                                     zc=zc, s0=start, s1=stop:
                                     h.matmul(b.psum[bk][:], hjit[hb][:, gg, :],
                                              wc_sb[:, zc, :], start=s0, stop=s1),
                                     after=waits, tag="cpmm")
                hjit_busy[hb] = last_mm
            b.bank_prod[bk] = last_mm
            mm_toks.append(last_mm)
            if dh == 0:
                ctx.setdefault("x4st_toks", {})[tile] = b.consume(
                    "scalar", bk, lambda h, bk=bk, tile=tile: h.copy(
                        x4stash[:, tile, :], b.psum[bk][:]), tag="x4st")
            else:
                xt = xt_sb[tile % 2]
                tlx = s.emit("sync", lambda h, tile=tile, xt=xt: h.dma_start(
                    out=xt[:], in_=x_spill[tile]), after=[ctx.get("p4_xt_busy")],
                    dma=True, tag="x3re")
                t1 = b.consume("vector", bk, lambda h, bk=bk, tile=tile, xt=xt:
                               h.tensor_tensor(x4r[:, 512:1024], b.psum[bk][:],
                                               xt[:, 512:1024], op=ALU.add),
                               after=[tlx] + ctx.get("p4_x4r_busy", []),
                               tag="x4a1")
                t2 = s.emit("vector", lambda h, tile=tile, xt=xt: h.tensor_tensor(
                    x4r[:, 0:512], x4stash[:, tile, :], xt[:, 0:512], op=ALU.add),
                    after=[t1, ctx["x4st_toks"][tile]], tag="x4a2")
                ctx["p4_xt_busy"] = t2
                t_ln = b.spine_ln(x4r[:], x4r[:], [t2], tag=f"ln4_{tile}")
                oy = s.emit("sync", lambda h, tile=tile: h.dma_start(
                    out=ctx["y_out"].rearrange("(n p) d -> n p d", p=128)[tile],
                    in_=x4r[:]), after=[t_ln], dma=True, tag="yout")
                ctx["p4_x4r_busy"] = [oy]
                b.out_toks.append(oy)
        ctx["p4_wc_busy"] = mm_toks[-1]
    b.close_phase()
    ctx["p4_done"] = True


def kernel(**inputs):
    ws = prep_host(inputs)
    x_full = np.ascontiguousarray(inputs["L0_attn_x"], dtype=np.float32)
    b, ctx = build()
    in_maps = []
    for core in range(8):
        m = {"x": x_full[core]}
        m.update(ws)
        in_maps.append(m)
    res = run_bass_kernel_spmd(b.nc, in_maps, core_ids=list(range(8)))
    return np.stack([r["y"] for r in res.results], axis=0)


# version 03 (final)
# speedup vs baseline: 1.1749x; 1.1749x over previous
import numpy as np
import ml_dtypes
import concourse.bass as bass
import concourse.mybir as mybir
from concourse.bass_utils import run_bass_kernel_spmd

F32 = mybir.dt.float32
BF16 = mybir.dt.bfloat16
AF = mybir.ActivationFunctionType
ALU = mybir.AluOpType
AX = mybir.AxisListType
BF = ml_dtypes.bfloat16

T = 2048
D = 1024
NT = T // 128
KC = D // 128
HH = 64
M = 16
E = 17

ENGINES = ("sync", "tensor", "vector", "scalar", "gpsimd")


class _Op:
    __slots__ = ("eng", "fn", "waits", "need_inc", "inc_amt", "tag", "lane")

    def __init__(self, eng, fn, waits, inc_amt, tag, lane):
        self.eng, self.fn, self.waits = eng, fn, waits
        self.need_inc = False
        self.inc_amt = inc_amt
        self.tag = tag
        self.lane = lane


class Tok:
    __slots__ = ("op",)

    def __init__(self, op):
        self.op = op


N_LANES = {"sync": 8, "gpsimd": 4, "scalar": 2, "vector": 2, "tensor": 1}


class Sched:

    def __init__(self, nc):
        self.nc = nc
        self.ops = {e: [] for e in ENGINES}
        self.lane_rr = {e: 0 for e in ENGINES}

    def emit(self, eng, fn, after=(), dma=False, tag=""):
        lane = None
        if dma:
            i = self.lane_rr[eng]
            self.lane_rr[eng] = (i + 1) % N_LANES[eng]
            lane = (eng, i)
        waits = []
        for t in after:
            if t is None:
                continue
            if t.op.eng == eng and t.op.lane is None:
                continue
            waits.append(t)
            t.op.need_inc = True
        op = _Op(eng, fn, waits, 16 if dma else 1, tag, lane)
        if dma:
            op.need_inc = True
        self.ops[eng].append(op)
        return Tok(op)

    def last(self, eng):
        return Tok(self.ops[eng][-1]) if self.ops[eng] else None

    def lasts_all(self):
        out = []
        for e in ENGINES:
            seen_lanes = set()
            for op in reversed(self.ops[e]):
                if op.lane is None:
                    out.append(Tok(op))
                    break
            for op in reversed(self.ops[e]):
                if op.lane is not None and op.lane not in seen_lanes:
                    seen_lanes.add(op.lane)
                    out.append(Tok(op))
        return out

    def barrier(self):
        lasts = self.lasts_all()
        toks = []
        for e in ENGINES:
            toks.append(self.emit(e, lambda h: h.nop(), after=lasts,
                                  tag="barrier"))
        return toks

    def finalize(self, final_waits):
        nc = self.nc
        for t in final_waits:
            t.op.need_inc = True
        val = {}
        cnt = {}
        for e in ENGINES:
            for op in self.ops[e]:
                key = op.lane if op.lane is not None else e
                if op.need_inc or op.lane is not None:
                    cnt[key] = cnt.get(key, 0) + op.inc_amt
                val[op] = cnt.get(key, 0)
        import contextlib
        stack = contextlib.ExitStack()
        sems = {}
        for e in ENGINES:
            sems[e] = stack.enter_context(nc.semaphore(f"sem_{e}"))
            for i in range(N_LANES[e]):
                sems[(e, i)] = stack.enter_context(nc.semaphore(f"dl_{e}{i}"))

        def run(h, eng, finals=()):
            waited = {}
            lane_issued = {}
            for op in self.ops[eng]:
                for t in op.waits:
                    key = t.op.lane if t.op.lane is not None else t.op.eng
                    v = val[t.op]
                    if waited.get(key, 0) < v:
                        h.wait_ge(sems[key], v)
                        waited[key] = v
                if op.lane is not None:
                    prior = lane_issued.get(op.lane, 0)
                    if prior and waited.get(op.lane, 0) < prior:
                        h.wait_ge(sems[op.lane], prior)
                        waited[op.lane] = prior
                    lane_issued[op.lane] = val[op]
                inst = op.fn(h)
                if op.need_inc or op.lane is not None:
                    key = op.lane if op.lane is not None else eng
                    if inst is None:
                        h.nop().then_inc(sems[key], op.inc_amt)
                    else:
                        inst.then_inc(sems[key], op.inc_amt)
            for t in finals:
                key = t.op.lane if t.op.lane is not None else t.op.eng
                v = val[t.op]
                if waited.get(key, 0) < v:
                    h.wait_ge(sems[key], v)
                    waited[key] = v

        with nc.Block() as block:
            @block.sync
            def _(h):
                run(h, "sync", finals=final_waits)

            @block.tensor
            def _(h):
                run(h, "tensor")

            @block.vector
            def _(h):
                run(h, "vector")

            @block.scalar
            def _(h):
                run(h, "scalar")

            @block.gpsimd
            def _(h):
                run(h, "gpsimd")
        stack.close()


def prep_host(inputs):
    f32 = np.float32
    ws = {}

    def c(x):
        return np.ascontiguousarray(x, dtype=f32)

    ws["norm_w_rep"] = c(np.broadcast_to(inputs["L0_attn_norm_w"], (128, D)))
    ws["WvT"] = c(inputs["L0_attn_W_v"].T)
    ws["WaT"] = c(inputs["L0_attn_W_a"].T)
    ws["WoutT"] = c(inputs["L0_attn_W_out"].T)
    ws["We"] = c(inputs["L0_ffn_We"])
    ws["be_cols"] = c(inputs["L0_ffn_be"].reshape(20, 8, 128).transpose(0, 2, 1))
    ws["Wq0"] = c(inputs["L0_ffn_Wq"])
    ws["Wk0"] = c(inputs["L0_ffn_Wk"])
    ws["Wv0"] = c(inputs["L0_ffn_Wv"])
    ws["Wo0"] = c(inputs["L0_ffn_Wo"])
    ws["ew_rep"] = c(np.broadcast_to(inputs["L0_ffn_ew"].reshape(1, 18), (128, 18)))
    ws["actw_rep"] = c(np.broadcast_to(inputs["L0_ffn_act_w"].reshape(1, 6), (128, 6)))
    ws["ln_g_rep"] = c(np.broadcast_to(inputs["L0_ffn_ln_g"], (128, D)))
    ws["ln_b_rep"] = c(np.broadcast_to(inputs["L0_ffn_ln_b"], (128, D)))
    for nm, bn in (("Wq", "bq"), ("Wk", "bk"), ("Wv", "bv")):
        w = inputs[f"L1_attn_{nm}"]
        ws[nm + "1"] = c(w.transpose(1, 0, 2).reshape(D, D))
        bias = inputs[f"L1_attn_{bn}"].reshape(D)
        ws[bn + "1"] = c(bias.reshape(8, 128).T)
    ws["Wo1"] = c(inputs["L1_attn_Wo"])
    ws["bo1_rep"] = c(np.broadcast_to(inputs["L1_attn_bo"], (128, D)))
    rm = inputs["L1_ffn_rail_maker"]
    Wyp = np.zeros((4, 8, 128, 128), f32)
    for h in range(4):
        for r in range(64):
            rc, ro = r // 8, r % 8
            Wyp[h, rc, ro*16:(ro+1)*16, ro*16:(ro+1)*16] = rm[h, r]
    ws["Wyp"] = Wyp
    ru = inputs["L1_ffn_rail_up_proj"]
    Wzp = np.zeros((64, 128, 128), f32)
    for h in range(4):
        for p in range(32):
            t, base = ZMAP[(h, p)]
            q = p // 2
            for rr in range(2):
                r = 2 * p + rr
                row0 = base + (r - 4 * q) * 16
                Wzp[t, row0:row0 + 16, rr * 64:(rr + 1) * 64] = ru[h, r].T
    ws["Wzp"] = Wzp
    ws["WcT"] = c(inputs["L1_ffn_w_cproj"].T)
    ws["ident_bf"] = np.eye(128, dtype=f32).astype(BF)
    ws["ident_f32"] = np.eye(128, dtype=f32)
    mask = np.zeros((128, 128), f32)
    for i in range(128):
        mask[i, i + 1:] = -1e9
    ws["mask128"] = mask
    return ws


class B:
    def __init__(self, dbg_taps=(), phases=("p1", "p2", "p3", "p4")):
        self.nc = bass.Bass("TRN2", target_bir_lowering=False)
        self.s = Sched(self.nc)
        import contextlib
        self.stack = contextlib.ExitStack()
        self.phase_stack = None
        self.dbg_taps = set(dbg_taps)
        self.dbg_tokens = []
        self.phases = phases
        self.out_toks = []
        self.psum = [self.stack.enter_context(
            self.nc.psum_tensor(f"psb{i}", [128, 512], F32)) for i in range(8)]
        self.bank_prod = [None] * 8
        self.bank_cons = [None] * 8
        self._rr = 0

    def din(self, name, shape, dtype=F32):
        return self.nc.dram_tensor(name, list(shape), dtype, kind="ExternalInput").ap()

    def dout(self, name, shape, dtype=F32):
        return self.nc.dram_tensor(name, list(shape), dtype, kind="ExternalOutput").ap()

    def dscratch(self, name, shape, dtype):
        return self.nc.dram_tensor(name, list(shape), dtype, kind="Internal").ap()

    def sbuf(self, name, shape, dtype, phase=False):
        st = self.phase_stack if phase else self.stack
        return st.enter_context(self.nc.sbuf_tensor(name, list(shape), dtype))

    def open_phase(self):
        import contextlib
        assert self.phase_stack is None
        self.phase_stack = contextlib.ExitStack()

    def close_phase(self):
        self.phase_stack.close()
        self.phase_stack = None
        self.s.barrier()

    def bank(self):
        b = self._rr
        self._rr = (self._rr + 1) % 8
        return b

    def mm_acc(self, bankid, pieces, out_ap=None, tag="mm"):
        n = len(pieces)
        tok = None
        for i, (lf, rf, aft) in enumerate(pieces):
            start, stop = i == 0, i == n - 1
            waits = list(aft)
            if start and self.bank_cons[bankid] is not None:
                waits.append(self.bank_cons[bankid])
            oa = out_ap if out_ap is not None else self.psum[bankid][:]
            tok = self.s.emit("tensor", lambda h, lf=lf, rf=rf, s0=start, s1=stop,
                              oa=oa: h.matmul(oa, lf(), rf(), start=s0, stop=s1),
                              after=waits, tag=tag)
        self.bank_prod[bankid] = tok
        return tok

    def consume(self, eng, bankid, fn, after=(), tag="evac"):
        waits = list(after)
        if self.bank_prod[bankid] is not None:
            waits.append(self.bank_prod[bankid])
        tok = self.s.emit(eng, fn, after=waits, tag=tag)
        self.bank_cons[bankid] = tok
        return tok

    def transpose_group(self, bankid, pieces, bf16=True, tag="tr"):
        tok = None
        first = True
        for of, inf, aft in pieces:
            waits = list(aft)
            if first and self.bank_cons[bankid] is not None:
                waits.append(self.bank_cons[bankid])
            first = False
            ba = self.psum[bankid][:].bitcast(BF16) if bf16 else self.psum[bankid][:]
            tok = self.s.emit("tensor", lambda h, of=of, inf=inf, ba=ba:
                              h.transpose(of(ba), inf(), self.ident()),
                              after=waits, tag=tag)
        self.bank_prod[bankid] = tok
        return tok

    def ident(self):
        return self.ident_sb[:]

    def identf(self):
        return self.identf_sb[:]

    def dbg(self, name, ap_fn, shape, dtype, after):
        if name not in self.dbg_taps:
            return
        o = self.dout("dbg_" + name, shape, dtype)
        self.dbg_tokens.append(self.s.emit(
            "sync", lambda h: h.dma_start(out=o, in_=ap_fn()),
            after=after, dma=True, tag="dbg_" + name))


def tm_to_fm(b, src_ap_fn, nchunks, dest_fn, after, src_bf16, evac_eng="scalar"):
    toks = []
    per = 8 if src_bf16 else 4
    i = 0
    while i < nchunks:
        cnt = min(per, nchunks - i)
        bk = b.bank()
        pieces = []
        for c in range(i, i + cnt):
            def of(ba, c=c, i=i, src_bf16=src_bf16):
                if src_bf16:
                    return ba.rearrange("p (n m) -> p n m", m=128)[:, c - i, :]
                return None
            if src_bf16:
                pieces.append((of, (lambda c=c: src_ap_fn(c)), after))
            else:
                pieces.append((None, None, None))
        if src_bf16:
            tok_p = b.transpose_group(bk, pieces, bf16=True)
            def ev(h, bk=bk, i=i, cnt=cnt):
                return h.copy(dest_fn(i, cnt),
                              b.psum[bk][:].bitcast(BF16).rearrange(
                                  "p (n m) -> p n m", m=128)[:, 0:cnt, :])
            toks.append(b.consume(evac_eng, bk, ev))
        else:
            first = True
            tok_p = None
            for c in range(i, i + cnt):
                waits = list(after)
                if first and b.bank_cons[bk] is not None:
                    waits.append(b.bank_cons[bk])
                first = False
                tok_p = b.s.emit("tensor", lambda h, c=c, i=i, bk=bk:
                                 h.transpose(b.psum[bk][:].rearrange(
                                     "p (n m) -> p n m", m=128)[:, c - i, :],
                                     src_ap_fn(c), b.identf()),
                                 after=waits, tag="trf")
            b.bank_prod[bk] = tok_p
            def ev(h, bk=bk, i=i, cnt=cnt):
                return h.copy(dest_fn(i, cnt),
                              b.psum[bk][:].rearrange(
                                  "p (n m) -> p n m", m=128)[:, 0:cnt, :])
            toks.append(b.consume(evac_eng, bk, ev))
        i += cnt
    return toks


def build(dbg_taps=(), phases=("p1", "p2", "p3", "p4")):
    b = B(dbg_taps, phases)
    nc, s = b.nc, b.s

    x_in = b.din("x", (T, D))
    y_out = b.dout("y", (T, D))
    g = {}
    for name, shape, dt in (
        ("norm_w_rep", (128, D), F32), ("WvT", (D, D), F32),
        ("WaT", (D, HH * M * E), F32), ("WoutT", (D, D), F32),
        ("We", (20, D, D), F32), ("be_cols", (20, 128, 8), F32),
        ("Wq0", (6, D, D), F32), ("Wk0", (6, D, D), F32),
        ("Wv0", (6, D, D), F32), ("Wo0", (6, D, D), F32),
        ("ew_rep", (128, 18), F32), ("actw_rep", (128, 6), F32),
        ("ln_g_rep", (128, D), F32), ("ln_b_rep", (128, D), F32),
        ("Wq1", (D, D), F32), ("Wk1", (D, D), F32), ("Wv1", (D, D), F32),
        ("bq1", (128, 8), F32), ("bk1", (128, 8), F32), ("bv1", (128, 8), F32),
        ("Wo1", (D, D), F32), ("bo1_rep", (128, D), F32),
        ("Wyp", (4, 8, 128, 128), F32), ("Wzp", (64, 128, 128), F32),
        ("WcT", (16384, D), F32),
        ("ident_bf", (128, 128), BF16), ("ident_f32", (128, 128), F32),
        ("mask128", (128, 128), F32),
    ):
        g[name] = b.din(name, shape, dt)

    _ds = b.dout if "spill" in b.dbg_taps else b.dscratch
    x_spill = _ds("x_spill", (NT, 128, D), F32)
    exp_spill = _ds("exp_spill", (NT, 128, HH * M * E), BF16)
    u_spill = _ds("u_spill", (NT, 128, D), F32)
    rc_spill = _ds("rc_spill", (NT, 128, D), F32)
    node_spill = _ds("node_spill", (7, KC, 128, T), BF16)
    q_spill = _ds("q_spill", (KC, 128, T), F32)
    h_spill = b.dscratch("h_spill", (NT, 128, 128, 128), BF16)

    b.ident_sb = b.sbuf("ident_sb", [128, 128], BF16)
    b.identf_sb = b.sbuf("identf_sb", [128, 128], F32)
    xt_sb = [b.sbuf(f"xt{i}", [128, D], F32) for i in range(2)]
    stat = b.sbuf("stat", [128, 8], F32)
    tmb = b.sbuf("tmb", [128, D], BF16)

    consts = []
    for dst, src in ((b.ident_sb, "ident_bf"), (b.identf_sb, "ident_f32")):
        consts.append(s.emit("sync", lambda h, d=dst, sr=src:
                             h.dma_start(out=d[:], in_=g[sr]),
                             dma=True, tag="const"))
    ctok = consts[-1]
    b.consts = consts

    b._ln_prev = None

    def spine_ln(src_ap, dst_ap, after, eps=1e-5, tag="ln", scratch_guard=()):
        aft0 = list(after) + list(scratch_guard) + [b._ln_prev]
        t1 = s.emit("scalar", lambda h: h.activation(
            tmb[:], src_ap, AF.Square, accum_out=stat[:, 0:1]),
            after=aft0, tag=tag + "_sq")
        t2 = s.emit("scalar", lambda h: h.activation(
            tmb[:], src_ap, AF.Copy, accum_out=stat[:, 1:2]),
            after=aft0, tag=tag + "_sum")
        t3 = s.emit("vector", lambda h: h.tensor_scalar(
            stat[:, 2:3], stat[:, 1:2], 1.0 / D, None, op0=ALU.mult),
            after=[t2], tag=tag + "_mu")
        t4 = s.emit("scalar", lambda h: h.activation(
            stat[:, 3:4], stat[:, 2:3], AF.Square), after=[t3], tag=tag + "_mu2")
        t5 = s.emit("vector", lambda h: h.tensor_scalar(
            stat[:, 4:5], stat[:, 0:1], 1.0 / D, None, op0=ALU.mult),
            after=[t1], tag=tag + "_ex2")
        t5 = s.emit("vector", lambda h: h.drain(), after=[t5], tag=tag + "_d5")
        t6 = s.emit("vector", lambda h: h.tensor_tensor(
            stat[:, 5:6], stat[:, 4:5], stat[:, 3:4], op=ALU.subtract),
            after=[t4, t5], tag=tag + "_var")
        t6 = s.emit("vector", lambda h: h.drain(), after=[t6], tag=tag + "_d6")
        t7 = s.emit("vector", lambda h: h.tensor_scalar(
            stat[:, 5:6], stat[:, 5:6], 1.0, eps, op0=ALU.mult, op1=ALU.add),
            after=[t6], tag=tag + "_vare")
        t8 = s.emit("scalar", lambda h: h.activation(
            stat[:, 6:7], stat[:, 5:6], AF.Ln), after=[t7], tag=tag + "_ln")
        t9 = s.emit("scalar", lambda h: h.activation(
            stat[:, 7:8], stat[:, 6:7], AF.Exp, scale=-0.5),
            after=[t8], tag=tag + "_rstd")
        t10 = s.emit("vector", lambda h: h.tensor_scalar(
            dst_ap, src_ap, stat[:, 2:3], stat[:, 7:8],
            op0=ALU.subtract, op1=ALU.mult), after=[t9], tag=tag + "_apply")
        b._ln_prev = t10
        return t10

    b.spine_ln = spine_ln
    ctx = dict(b=b, s=s, g=g, x_in=x_in, y_out=y_out, xt_sb=xt_sb,
               stat=stat, tmb=tmb, ctok=ctok, x_spill=x_spill,
               exp_spill=exp_spill, u_spill=u_spill, rc_spill=rc_spill,
               node_spill=node_spill, q_spill=q_spill, h_spill=h_spill)

    if "p1" in phases:
        p1(ctx)
    for ph in phases:
        if ph.startswith("p2n"):
            ctx["p2_nodes"] = int(ph[3:])
        if ph == "p2bar":
            ctx["p2_bar"] = True
    if "p2" in phases or any(ph.startswith("p2n") for ph in phases) or "p2bar" in phases:
        p2(ctx)
    if "p3" in phases:
        p3(ctx)
    if "p4" in phases:
        p4(ctx)

    finals = list(b.dbg_tokens) + b.out_toks
    if not finals:
        finals = [ctok]
    s.finalize(finals)
    return b, ctx


def p1(ctx):
    b = ctx["b"]; s = ctx["s"]; g = ctx["g"]
    xt_sb = ctx["xt_sb"]; stat = ctx["stat"]
    tmb = ctx["tmb"]
    x_in = ctx["x_in"]; ctok = ctx["ctok"]
    exp_spill = ctx["exp_spill"]; u_spill = ctx["u_spill"]
    rc_spill = ctx["rc_spill"]; x_spill = ctx["x_spill"]
    RG = 272 * 8
    EPS = 1e-5

    b.open_phase()
    hfm = b.sbuf("hfm", [128, KC, T], BF16, phase=True)
    normw_sb = b.sbuf("normw_sb", [128, D], F32, phase=True)
    wv_sb = b.sbuf("wv_sb", [128, KC, D], BF16, phase=True)
    wa_sb = [b.sbuf(f"wa{i}", [128, KC, RG], BF16, phase=True) for i in range(2)]
    logit = b.sbuf("logit", [128, RG], F32, phase=True)
    expb = b.sbuf("expb", [128, RG], BF16, phase=True)
    nmax = b.sbuf("nmax", [128, 128], F32, phase=True)
    denom = b.sbuf("denom", [128, 128], F32, phase=True)
    recip = b.sbuf("recip", [128, 128], F32, phase=True)
    uh = b.sbuf("uh", [128, 128], F32, phase=True)
    vh_sb = b.sbuf("vh_sb", [128, 128], BF16, phase=True)
    x3 = x_in.rearrange("(n p) d -> n p d", p=128)
    tok_nw = s.emit("sync", lambda h: h.dma_start(
        out=normw_sb[:], in_=g["norm_w_rep"]), dma=True, tag="nwload")

    tile_busy = [None, None]
    tmb_busy = None
    for t in range(NT):
        xt = xt_sb[t % 2]
        tl = s.emit("sync", lambda h, t=t, xt=xt: h.dma_start(
            out=xt[:], in_=x3[t]), after=[tile_busy[t % 2]], dma=True,
            tag=f"xload{t}")
        t_sq = s.emit("scalar", lambda h, xt=xt: h.activation(
            tmb[:], xt[:], AF.Square, accum_out=stat[:, 0:1]),
            after=[tl, tmb_busy], tag="sq")
        t_v = s.emit("vector", lambda h: h.tensor_scalar(
            stat[:, 1:2], stat[:, 0:1], 1.0 / D, EPS, op0=ALU.mult, op1=ALU.add),
            after=[t_sq], tag="rmsv")
        t_l = s.emit("scalar", lambda h: h.activation(
            stat[:, 2:3], stat[:, 1:2], AF.Ln), after=[t_v], tag="rmsln")
        t_r = s.emit("scalar", lambda h: h.activation(
            stat[:, 3:4], stat[:, 2:3], AF.Exp, scale=-0.5),
            after=[t_l], tag="rmsr")
        t_h = s.emit("vector", lambda h, xt=xt: h.scalar_tensor_tensor(
            tmb[:], xt[:], stat[:, 3:4], normw_sb[:],
            op0=ALU.mult, op1=ALU.mult), after=[t_r, tok_nw, ctok], tag="happly")
        tile_busy[t % 2] = t_h
        bk = b.bank()
        pieces = [((lambda ba, c=c: ba.rearrange("p (n m) -> p n m", m=128)[:, c, :]),
                   (lambda c=c: tmb[:, c * 128:(c + 1) * 128]), [t_h, ctok])
                  for c in range(KC)]
        tmb_busy = b.transpose_group(bk, pieces, bf16=True, tag="htr")
        te = b.consume("scalar", bk, lambda h, bk=bk, t=t: h.copy(
            hfm[:, :, t * 128:(t + 1) * 128],
            b.psum[bk][:].bitcast(BF16).rearrange("p (n m) -> p n m", m=128)),
            tag="hevac")
        ctx.setdefault("tok_hfm", [None] * NT)[t] = te

    tok_hfm = ctx["tok_hfm"]
    b.dbg("hfm", lambda: hfm[:], (128, KC, T), BF16, tok_hfm)

    tok_wv = []
    for k in range(KC):
        tok_wv.append(s.emit("gpsimd", lambda h, k=k: h.dma_start(
            out=wv_sb[:, k, :], in_=g["WvT"][k * 128:(k + 1) * 128, :]),
            dma=True, tag="wvload"))

    tok_wa = [[None] * KC for _ in range(2)]
    wa_free = [None, None]
    prev_sp = [None, None, None]
    prev_tmax = None
    prev_tu1 = None
    for rg in range(8):
        buf = rg % 2
        for k in range(KC):
            tok_wa[buf][k] = s.emit("gpsimd", lambda h, k=k, rg=rg, buf=buf:
                                    h.dma_start(out=wa_sb[buf][:, k, :],
                                                in_=g["WaT"][k * 128:(k + 1) * 128,
                                                             rg * RG:(rg + 1) * RG]),
                                    after=[wa_free[buf]], dma=True,
                                    tag=f"waload{rg}")
        for t in range(NT):
            mm_toks = []
            for rc in range(8):
                bk = rc
                pieces = []
                for k in range(KC):
                    pieces.append((
                        (lambda k=k, t=t: hfm[:, k, t * 128:(t + 1) * 128]),
                        (lambda k=k, rc=rc, buf=buf: wa_sb[buf][:, k, rc * 272:(rc + 1) * 272]),
                        [tok_hfm[t], tok_wa[buf][k]]))
                b.mm_acc(bk, pieces, out_ap=b.psum[bk][:, 0:272], tag="gamm")
                mm_toks.append(b.bank_prod[bk])
            wa_free[buf] = mm_toks[-1]
            ev_toks = []
            for rc in range(8):
                aft = [prev_tmax] if rc == 0 else []
                ev_toks.append(b.consume("scalar", rc, lambda h, rc=rc:
                                         h.copy(logit[:, rc * 272:(rc + 1) * 272],
                                                b.psum[rc][:, 0:272]),
                                         after=aft, tag="gaevac"))
            l3 = logit[:].rearrange("p (gg e) -> p gg e", e=E)
            t_max = s.emit("vector", lambda h, l3=l3: h.reduce_max(
                nmax[:].rearrange("p (gg o) -> p gg o", o=1), l3, axis=AX.X,
                negate=True), after=ev_toks, tag="gamax")
            prev_tmax = t_max
            nb = nmax[:].rearrange("p (gg o) -> p gg o", o=1).broadcast_to([128, 128, E])
            t_sub = s.emit("vector", lambda h, l3=l3, nb=nb: h.tensor_tensor(
                expb[:].rearrange("p (gg e) -> p gg e", e=E), l3, nb, op=ALU.add),
                after=[t_max, prev_sp[0]], tag="gasub")
            t_exp = s.emit("scalar", lambda h: h.activation(
                expb[:], expb[:], AF.Exp), after=[t_sub], tag="gaexp")
            t_den = s.emit("vector", lambda h: h.reduce_sum(
                denom[:].rearrange("p (gg o) -> p gg o", o=1),
                expb[:].rearrange("p (gg e) -> p gg e", e=E), axis=AX.X),
                after=[t_exp], tag="gaden")
            t_den = s.emit("vector", lambda h: h.drain(), after=[t_den], tag="gadr")
            t_rec = s.emit("vector", lambda h: h.reciprocal(
                recip[:], denom[:]), after=[t_den, prev_sp[2]], tag="garec")
            vpieces = [((lambda k=k, t=t: hfm[:, k, t * 128:(t + 1) * 128]),
                        (lambda k=k, rg=rg: wv_sb[:, k, rg * 128:(rg + 1) * 128]),
                        [tok_hfm[t], tok_wv[k]]) for k in range(KC)]
            b.mm_acc(0, vpieces, out_ap=b.psum[0][:, 0:128], tag="vmm")
            t_v = b.consume("scalar", 0, lambda h: h.copy(
                vh_sb[:], b.psum[0][:, 0:128]), after=[prev_tu1], tag="vevac")
            t_u1 = s.emit("vector", lambda h, rg=rg, t=t: h.tensor_tensor(
                uh[:], expb[:, 0::E], vh_sb[:],
                op=ALU.mult), after=[t_exp, t_v, prev_sp[1]], tag="gau1")
            t_u2 = s.emit("vector", lambda h: h.tensor_tensor(
                uh[:], uh[:], recip[:], op=ALU.mult), after=[t_rec, t_u1], tag="gau2")
            sp1 = s.emit("sync", lambda h, t=t, rg=rg: h.dma_start(
                out=exp_spill[t, :, rg * RG:(rg + 1) * RG], in_=expb[:]),
                after=[t_exp], dma=True, tag="spill_exp")
            sp2 = s.emit("sync", lambda h, t=t, rg=rg: h.dma_start(
                out=u_spill[t, :, rg * 128:(rg + 1) * 128], in_=uh[:]),
                after=[t_u2], dma=True, tag="spill_u")
            sp3 = s.emit("sync", lambda h, t=t, rg=rg: h.dma_start(
                out=rc_spill[t, :, rg * 128:(rg + 1) * 128], in_=recip[:]),
                after=[t_rec], dma=True, tag="spill_rc")
            prev_sp = [sp1, sp2, sp3]
            prev_tu1 = t_u1
    b.close_phase()

    b.open_phase()
    A_sc_h = b.sbuf("A_sc", [64, 256, 128], BF16, phase=True)
    A_sc = A_sc_h[:]
    u_sc = b.sbuf("u_sc", [64, 16, 128], F32, phase=True)[:]
    r_sc = b.sbuf("r_sc", [64, 16, 128], BF16, phase=True)[:]
    y_sc = b.sbuf("y_sc", [64, 16, 128], F32, phase=True)[:]
    prod_h = b.sbuf("prod", [64, 256], F32, phase=True)
    prod = prod_h[:]
    praw = b.sbuf("praw", [64, 16], F32, phase=True)[:]
    praw2 = b.sbuf("praw2", [64, 16], F32, phase=True)[:]
    st_sb = b.sbuf("st_sb", [64, 16], F32, phase=True)[:]
    expin = [b.sbuf("expin0", [128, HH * M * E], BF16, phase=True)]
    uin = [b.sbuf("uin0", [128, D], F32, phase=True)]
    rcin = [b.sbuf("rcin0", [128, D], F32, phase=True)]
    ytm = b.sbuf("ytm", [128, D], BF16, phase=True)
    yfmt = b.sbuf("yfmt", [128, KC, 128], BF16, phase=True)
    wout_sb = b.sbuf("wout_sb", [128, KC, D], BF16, phase=True)
    x1t = b.sbuf("x1t", [128, D], F32, phase=True)
    x1fmt = b.sbuf("x1fmt", [128, KC, 128], BF16, phase=True)
    node_spill = ctx["node_spill"]

    tok_wout = []
    for k in range(KC):
        tok_wout.append(s.emit("gpsimd", lambda h, k=k: h.dma_start(
            out=wout_sb[:, k, :], in_=g["WoutT"][k * 128:(k + 1) * 128, :]),
            dma=True, tag="woutload"))

    t_zero = s.emit("vector", lambda h: h.memset(st_sb, 0.0), tag="szero")

    in_busy = [None]
    xt_busy = [None, None]
    x1t_busy = []
    tmb_busy = None
    tok_x1spill = []
    prev_step = t_zero
    tok_xfm = [None] * NT
    dbg_x1 = b.dout("dbg_x1", (NT, 128, D)) if "x1" in b.dbg_taps else None
    for t in range(NT):
        bufi = 0
        li1 = s.emit("sync", lambda h, t=t, bufi=bufi: h.dma_start(
            out=expin[bufi][:], in_=exp_spill[t]), after=[in_busy[bufi]],
            dma=True, tag="ld_exp")
        li2 = s.emit("sync", lambda h, t=t, bufi=bufi: h.dma_start(
            out=uin[bufi][:], in_=u_spill[t]), after=[in_busy[bufi]],
            dma=True, tag="ld_u")
        li3 = s.emit("sync", lambda h, t=t, bufi=bufi: h.dma_start(
            out=rcin[bufi][:], in_=rc_spill[t]), after=[in_busy[bufi]],
            dma=True, tag="ld_rc")
        prep_toks = []
        rtoks = [None] * 4
        for which, src, dst, ltok in ((0, uin, u_sc, li2), (1, rcin, r_sc, li3)):
            for grp in range(4):
                bk = b.bank()
                first = True
                tok_p = None
                for ii in range(4):
                    i = grp * 4 + ii
                    waits = [ltok, prev_step]
                    if first and b.bank_cons[bk] is not None:
                        waits.append(b.bank_cons[bk])
                    first = False
                    tok_p = s.emit(
                        "tensor", lambda h, i=i, ii=ii, bk=bk, bufi=bufi, src=src:
                        h.transpose(b.psum[bk][:].rearrange(
                            "p (n m) -> p n m", m=128)[0:64, ii, :],
                            src[bufi][:].rearrange(
                                "p (hh i) -> p hh i", i=16)[:, :, i],
                            b.identf()), after=waits, tag="utr")
                b.bank_prod[bk] = tok_p
                tk = b.consume(
                    "scalar", bk, lambda h, bk=bk, grp=grp, dst=dst:
                    h.copy(dst[:, grp * 4:grp * 4 + 4, :],
                           b.psum[bk][:].rearrange(
                               "p (n m) -> p n m", m=128)[0:64]),
                    tag="uevac")
                prep_toks.append(tk)
                if which == 1:
                    rtoks[grp] = tk
        for i in range(M):
            for jg in range(2):
                bk = b.bank()
                pieces = []
                for jj in range(8):
                    j = jg * 8 + jj
                    def of(ba, jj=jj):
                        return ba.rearrange("p (n m) -> p n m", m=128)[0:64, jj, :]
                    def inf(i=i, j=j, bufi=bufi):
                        col = i * E + (j + 1)
                        return expin[bufi][:].rearrange(
                            "p (hh r) -> p hh r", r=272)[:, :, col]
                    pieces.append((of, inf, [li1, prev_step]))
                b.transpose_group(bk, pieces, bf16=True, tag="Atr")
                def ev(h, bk=bk, i=i, jg=jg):
                    return h.tensor_tensor(
                        A_sc[:, i * 16 + jg * 8:i * 16 + jg * 8 + 8, :],
                        b.psum[bk][:].bitcast(BF16).rearrange(
                            "p (n m) -> p n m", m=128)[0:64, :, :],
                        r_sc[:, i, :].unsqueeze(1).broadcast_to([64, 8, 128]),
                        op=ALU.mult)
                prep_toks.append(b.consume("vector", bk, ev,
                                           after=[rtoks[i // 4]], tag="Aevac"))
        for tt in range(128):
            if tt == 0:
                in1_fn = lambda: st_sb.unsqueeze(1).broadcast_to([64, 16, 16])
                aft_m = prep_toks + [prev_step]
            else:
                in1_fn = (lambda tt=tt: y_sc[:, :, tt - 1]
                          .unsqueeze(1).broadcast_to([64, 16, 16]))
                aft_m = [prev_step]
            t_m = s.emit("vector", lambda h, tt=tt, in1_fn=in1_fn: h.tensor_tensor(
                prod.rearrange("p (a bb) -> p a bb", bb=16),
                A_sc[:, :, tt].rearrange("p (a bb) -> p a bb", bb=16),
                in1_fn(), op=ALU.mult), after=aft_m, tag="scm")
            t_r2 = s.emit("vector", lambda h: h.reduce_sum(
                praw.unsqueeze(2),
                prod.rearrange("p (a bb) -> p a bb", bb=16), axis=AX.X),
                after=[t_m], tag="scr")
            t_r2 = s.emit("vector", lambda h: h.drain(), after=[t_r2], tag="scdr")
            t_f2 = s.emit("vector", lambda h, tt=tt: h.tensor_tensor(
                y_sc[:, :, tt], praw, u_sc[:, :, tt], op=ALU.add),
                after=[t_r2], tag="scf2")
            t_f2 = s.emit("vector", lambda h: h.drain(), after=[t_f2], tag="scdr3")
            prev_step = t_f2
            if t == 0 and tt == 1 and "scanstep" in b.dbg_taps:
                for nm, ap_, shp in (("prod1", prod, (64, 256)),
                                     ("praw1", praw, (64, 16)),
                                     ("praw2d", praw2, (64, 16))):
                    o_ = b.dout("dbg_" + nm, shp, F32)
                    tk = s.emit("sync", lambda h, o_=o_, ap_=ap_: h.dma_start(
                        out=o_, in_=ap_), after=[t_f2], dma=True, tag="dbgss")
                    b.dbg_tokens.append(tk)
                prev_step = b.dbg_tokens[-1]
        prev_step = s.emit("vector", lambda h: h.tensor_copy(
            st_sb, y_sc[:, :, 127]), after=[prev_step], tag="scstash")
        prev_step = s.emit("vector", lambda h: h.drain(), after=[prev_step], tag="scdr4")
        in_busy[bufi] = prev_step
        scan_done = prev_step
        if t == 0 and "scan0" in b.dbg_taps:
            for nm, ap_, shp, dt_ in (("A0", A_sc, (64, 256, 128), BF16),
                                      ("u0", u_sc, (64, 16, 128), F32),
                                      ("r0", r_sc, (64, 16, 128), F32),
                                      ("y0", y_sc, (64, 16, 128), F32)):
                o_ = b.dout("dbg_" + nm, shp, dt_)
                b.dbg_tokens.append(s.emit(
                    "sync", lambda h, o_=o_, ap_=ap_: h.dma_start(out=o_, in_=ap_),
                    after=[scan_done], dma=True, tag="dbg" + nm))
            in_busy[bufi] = b.dbg_tokens[-1]
            prev_step = b.dbg_tokens[-1]

        ytm3 = ytm[:].rearrange("p (hh i) -> p hh i", i=16).transpose((0, 2, 1))
        ytm_toks = []
        for grp in range(2):
            bk = b.bank()
            first = True
            tok_p = None
            for ii in range(8):
                i = grp * 8 + ii
                waits = [scan_done]
                if first and b.bank_cons[bk] is not None:
                    waits.append(b.bank_cons[bk])
                first = False
                tok_p = s.emit("tensor", lambda h, i=i, ii=ii, bk=bk:
                               h.transpose(b.psum[bk][:].rearrange(
                                   "p (n m) -> p n m", m=64)[:, ii, :],
                                   y_sc[:, i, :], b.identf_sb[0:64, 0:64]),
                               after=waits, tag="ytr")
            b.bank_prod[bk] = tok_p
            ytm_toks.append(b.consume(
                "scalar", bk, lambda h, bk=bk, grp=grp:
                h.copy(ytm3[:, grp * 8:grp * 8 + 8, :],
                       b.psum[bk][:].rearrange("p (n m) -> p n m", m=64)),
                tag="ytmev"))
        bk = b.bank()
        pieces = [((lambda ba, c=c: ba.rearrange("p (n m) -> p n m", m=128)[:, c, :]),
                   (lambda c=c: ytm[:, c * 128:(c + 1) * 128]),
                   list(ytm_toks)) for c in range(KC)]
        b.transpose_group(bk, pieces, bf16=True, tag="yftr")
        yfm_toks = [b.consume("scalar", bk, lambda h, bk=bk: h.copy(
            yfmt[:],
            b.psum[bk][:].bitcast(BF16).rearrange("p (n m) -> p n m", m=128)),
            tag="yfmev")]
        xt = xt_sb[t % 2]
        tlx = s.emit("sync", lambda h, t=t, xt=xt: h.dma_start(
            out=xt[:], in_=x3[t]), after=[xt_busy[t % 2]], dma=True, tag="xre")
        half_toks = []
        for half in range(2):
            bk = b.bank()
            pieces = [((lambda k=k, t=t: yfmt[:, k, :]),
                       (lambda k=k, half=half: wout_sb[:, k, half * 512:(half + 1) * 512]),
                       yfm_toks + [tok_wout[k]]) for k in range(KC)]
            b.mm_acc(bk, pieces, tag="woutmm")
            half_toks.append(b.consume(
                "vector", bk, lambda h, bk=bk, half=half, xt=xt:
                h.scalar_tensor_tensor(x1t[:, half * 512:(half + 1) * 512],
                                       xt[:, half * 512:(half + 1) * 512], 2.0,
                                       b.psum[bk][:], op0=ALU.mult, op1=ALU.add),
                after=[tlx] + x1t_busy, tag="resadd"))
        xt_busy[t % 2] = half_toks[-1]
        t_ln = b.spine_ln(x1t[:], x1t[:], half_toks, tag="ln1",
                          scratch_guard=[tmb_busy])
        sp = s.emit("sync", lambda h, t=t: h.dma_start(
            out=x_spill[t], in_=x1t[:]), after=[t_ln], dma=True, tag="x1spill")
        tok_x1spill.append(sp)
        if "x1" in b.dbg_taps:
            spd = s.emit("sync", lambda h, t=t: h.dma_start(
                out=dbg_x1[t], in_=x1t[:]), after=[t_ln], dma=True, tag="dbgx1")
            b.dbg_tokens.append(spd)
            tok_x1spill.append(spd)
        t_cast = s.emit("scalar", lambda h: h.copy(tmb[:], x1t[:]),
                        after=[t_ln], tag="x1cast")
        x1t_busy = [sp, t_cast] + ([spd] if "x1" in b.dbg_taps else [])
        bk = b.bank()
        pieces = [((lambda ba, c=c: ba.rearrange(
            "p (n m) -> p n m", m=128)[:, c, :]),
            (lambda c=c: tmb[:, c * 128:(c + 1) * 128]),
            [t_cast]) for c in range(KC)]
        tmb_busy = b.transpose_group(bk, pieces, bf16=True, tag="x1tr")
        evt = b.consume("scalar", bk, lambda h, bk=bk: h.copy(
            x1fmt[:],
            b.psum[bk][:].bitcast(BF16).rearrange("p (n m) -> p n m", m=128)),
            after=[tok_xfm[t - 1] if t else None], tag="x1evac")
        tok_xfm[t] = s.emit("sync", lambda h, t=t: h.dma_start(
            out=node_spill[6, :, :, t * 128:(t + 1) * 128].transpose((1, 0, 2)) if False
            else node_spill[6].transpose((1, 0, 2))[:, :, t * 128:(t + 1) * 128],
            in_=x1fmt[:]), after=[evt], dma=True, tag="x1fmsp")
    ctx["tok_xfm_p1"] = tok_xfm
    ctx["tok_x1spill"] = tok_x1spill
    b.close_phase()
    ctx["p1_done"] = True


def zmap():
    m = {}
    for h in range(4):
        ie, io = 0, 0
        for p in range(32):
            q = p // 2
            if q % 2 == 0:
                m[(h, p)] = (h * 16 + ie, 0)
                ie += 1
            else:
                m[(h, p)] = (h * 16 + io, 64)
                io += 1
    return m


ZMAP = zmap()


EDGES = [
    (0, 0, 0, 6, 6, 6),
    (2, 1, 2, 0, 6, 0),
    (5, 3, 4, 1, 6, 0),
    (9, 6, 8, 2, 6, 1),
    (14, 10, 12, 3, 6, 1),
    (19, 15, 17, 4, 0, 2),
]
RSQ = float(1.0 / np.sqrt(128.0))


def p2(ctx):
    b = ctx["b"]; s = ctx["s"]; g = ctx["g"]
    stat = ctx["stat"]; tmb = ctx["tmb"]
    xt_sb = ctx["xt_sb"]
    node_spill = ctx["node_spill"]; q_spill = ctx["q_spill"]
    x_spill = ctx["x_spill"]
    LN_EPS = 1e-6

    b.open_phase()
    qh = b.sbuf("qh", [128, KC, T], BF16, phase=True)
    kh = b.sbuf("kh", [128, KC, T], BF16, phase=True)
    n5tm_a = kh[:, 0, 0:2048].bitcast(F32)

    vh = b.sbuf("vh", [128, NT, D], BF16, phase=True)
    Ofm = b.sbuf("Ofm", [128, KC, T], BF16, phase=True)
    wpool = [b.sbuf(f"wp{i}", [128, KC, D], BF16, phase=True) for i in range(2)]
    wbusy = [None, None]
    wtok = [None, None]
    wkey = [None, None]
    jit = [b.sbuf(f"jit{i}", [128, 512], BF16, phase=True) for i in range(10)]
    jit_busy = [None] * 10
    Pbuf = b.sbuf("Pbuf", [128, T], BF16, phase=True)
    q5tf = Pbuf[:].bitcast(F32)
    PT = b.sbuf("PT", [128, NT, 128], BF16, phase=True)
    nmax2 = b.sbuf("nmax2", [128, 8], F32, phase=True)
    ssum = b.sbuf("ssum", [128, 8], F32, phase=True)
    rcol = b.sbuf("rcol", [128, 1], F32, phase=True)
    bec = b.sbuf("bec", [128, 8], F32, phase=True)
    ew_sb = b.sbuf("ew_sb2", [128, 18], F32, phase=True)
    actw_sb = b.sbuf("actw_sb2", [128, 6], F32, phase=True)
    lng_sb = b.sbuf("lng_sb2", [128, D], BF16, phase=True)
    lnb_sb = b.sbuf("lnb_sb2", [128, D], BF16, phase=True)
    ncomb = [b.sbuf(f"ncomb{i}", [128, 512], BF16, phase=True) for i in range(2)]

    qstage_t = b.sbuf("qstage", [128, 4, 512], F32, phase=True)
    qstage = qstage_t[:, 0, :]
    qjit = [b.sbuf(f"qjit{i}", [128, 512], F32, phase=True) for i in range(2)]
    ncomb_busy = [None, None]
    n5tm = n5tm_a
    x2t = n5tm_a

    ctok2 = []
    for dst, src in ((ew_sb, "ew_rep"), (actw_sb, "actw_rep")):
        ctok2.append(s.emit("sync", lambda h, d=dst, sr=src: h.dma_start(
            out=d[:], in_=g[sr]), dma=True, tag="c2"))
    for dst, src in ((lng_sb, "ln_g_rep"), (lnb_sb, "ln_b_rep")):
        ctok2.append(s.emit("gpsimd", lambda h, d=dst, sr=src: h.dma_start(
            out=d[:], in_=g[sr]), dma=True, tag="c2"))
    ct2 = ctok2[-1]

    slot_done = {}

    jit_rr = [0]
    bec_busy = [None]
    qstage_busy = [None, None, None, None]
    qstage_rr = [0]

    def jit_load(src_ap, guard, after=(), tag="jit", width=512):
        bi = jit_rr[0]
        jit_rr[0] = (jit_rr[0] + 1) % 10
        tok = s.emit("sync", lambda h, src_ap=src_ap, bi=bi, width=width:
                     h.dma_start(out=jit[bi][:, 0:width], in_=src_ap),
            after=[jit_busy[bi], guard] + list(after), dma=True, tag=tag)
        return bi, tok

    wrr = [0]

    def wload_pool(src_ap, key, tag="wp"):
        for i in range(2):
            if wkey[i] == key:
                return i, wtok[i]
        i = wrr[0]
        wrr[0] = (wrr[0] + 1) % 2
        toks = []
        for k in range(KC):
            toks.append(s.emit("gpsimd", lambda h, i=i, k=k, src_ap=src_ap:
                               h.dma_start(out=wpool[i][:, k, :],
                                           in_=src_ap[k * 128:(k + 1) * 128, :]),
                               after=[wbusy[i]], dma=True, tag=tag))
        wkey[i] = key
        wtok[i] = toks[-1]
        return i, wtok[i]

    def edge_sweep(e_w, key, src_slot, j_scale, out_fn, tqs, tq0=0,
                   extra=(), tag="edge", bias=True, ev_extra=(), f32_spill=False):
        wi, wt = wload_pool(g["We"][e_w], ("We", e_w), tag=tag + "w")
        tl = s.emit("sync", lambda h: h.dma_start(
            out=bec[:], in_=g["be_cols"][e_w]), after=list(extra) + [bec_busy[0]],
            dma=True, tag=tag + "becl")
        tb = s.emit("vector", lambda h: h.tensor_scalar(
            bec[:], bec[:], ew_sb[:, j_scale:j_scale + 1], None, op0=ALU.mult),
            after=[tl, ct2], tag=tag + "becm")
        tb = s.emit("vector", lambda h: h.drain(), after=[tb], tag=tag + "becd")
        ev_toks = []
        for tq in range(tq0, tq0 + tqs):
            bufs = [jit_load(node_spill[src_slot, k, :, tq * 512:(tq + 1) * 512],
                             slot_done.get(src_slot), tag=tag + "jit")
                    for k in range(KC)]
            last_mm = None
            for oc in range(8):
                pieces = []
                for k in range(KC):
                    bi, ltok = bufs[k]
                    pieces.append((
                        (lambda k=k, oc=oc, wi=wi: wpool[wi][:, k, oc * 128:(oc + 1) * 128]),
                        (lambda bi=bi: jit[bi][:]), [wt, ltok, tb]))
                b.mm_acc(oc, pieces, tag=tag)
                last_mm = b.bank_prod[oc]
            for bi, _ in bufs:
                jit_busy[bi] = last_mm
            for oc in range(8):
                ev_toks.append(b.consume("scalar", oc,
                               lambda h, oc=oc, tq=tq: h.activation(
                                   out_fn(oc, tq), b.psum[oc][:], AF.Identity,
                                   bias=bec[:, oc:oc + 1],
                                   scale=ew_sb[:, j_scale:j_scale + 1]),
                               after=(list(ev_extra) if (tq == tq0 and oc == 0) else []),
                               tag=tag + "ev"))
                if f32_spill:
                    qi = qstage_rr[0]
                    qstage_rr[0] = (qi + 1) % 4
                    evf = b.consume("scalar", oc,
                                    lambda h, oc=oc, qi=qi: h.activation(
                                        qstage_t[:, qi, :], b.psum[oc][:], AF.Identity,
                                        bias=bec[:, oc:oc + 1],
                                        scale=ew_sb[:, j_scale:j_scale + 1]),
                                    after=[qstage_busy[qi]], tag=tag + "evf")
                    spq = s.emit("sync", lambda h, oc=oc, tq=tq, qi=qi: h.dma_start(
                        out=q_spill[oc, :, tq * 512:(tq + 1) * 512],
                        in_=qstage_t[:, qi, :]),
                        after=[evf, ctx.get("q_read_done")], dma=True,
                        tag=tag + "qspf")
                    qstage_busy[qi] = spq
                    ctx["_qspill_tok"] = spq
        wbusy[wi] = ev_toks[-1]
        bec_busy[0] = ev_toks[-1]
        return ev_toks

    def proj_sweep(w_name, w_idx, src_fn, out_fn, tqs, src_toks, tag="proj"):
        wi, wt = wload_pool(g[w_name][w_idx], (w_name, w_idx), tag=tag + "w")
        ev_toks = []
        last_mm = None
        for tq in range(tqs):
            for oc in range(8):
                pieces = [((lambda k=k, oc=oc, wi=wi: wpool[wi][:, k, oc * 128:(oc + 1) * 128]),
                           (lambda k=k, tq=tq: src_fn(k, tq)),
                           [wt] + list(src_toks)) for k in range(KC)]
                b.mm_acc(oc, pieces, tag=tag)
                last_mm = b.bank_prod[oc]
            for oc in range(8):
                ev_toks.append(b.consume("scalar", oc,
                               lambda h, oc=oc, tq=tq: h.copy(
                                   out_fn(oc, tq), b.psum[oc][:]), tag=tag + "ev"))
        wbusy[wi] = last_mm
        return ev_toks

    prev_pt = None
    prev_comb = None
    for c in range(ctx.get("p2_nodes", 6)):
        eq, ek, ev, sq, sk, sv = EDGES[c]
        awc = actw_sb[:, c:c + 1]

        q_evs = edge_sweep(eq, None, sq, c * 3 + 0,
                           lambda oc, tq: Ofm[:, oc, tq * 512:(tq + 1) * 512],
                           4, extra=[prev_comb], tag=f"qe{c}", f32_spill=True)
        qspt = ctx.pop("_qspill_tok")
        qh_evs = proj_sweep("Wq0", c,
                            lambda k, tq: Ofm[:, k, tq * 512:(tq + 1) * 512],
                            lambda oc, tq: qh[:, oc, tq * 512:(tq + 1) * 512],
                            4, q_evs, tag=f"qh{c}")

        kh_evs = []
        for qq in range(4):
            ke = edge_sweep(ek, None, sk, c * 3 + 1,
                            lambda oc, tq: Ofm[:, oc, 0:512],
                            1, tq0=qq, extra=qh_evs, tag=f"ke{c}_{qq}",
                            ev_extra=([qspt] if qq == 0 else []))
            kh_evs += proj_sweep("Wk0", c,
                                 lambda k, tq: Ofm[:, k, 0:512],
                                 lambda oc, tq, qq=qq: kh[:, oc, qq * 512:(qq + 1) * 512],
                                 1, ke, tag=f"kh{c}_{qq}")

        vh_evs = []
        for qq in range(4):
            ve = edge_sweep(ev, None, sv, c * 3 + 2,
                            lambda oc, tq: Ofm[:, oc, 0:512],
                            1, tq0=qq, extra=kh_evs, tag=f"ve{c}_{qq}")
            wiv, wtv = wload_pool(g["Wv0"][c], ("Wv0", c), tag=f"vh{c}w")
            last_mm = None
            for tt in range(4):
                gt = qq * 4 + tt
                for dv in range(2):
                    bk = (tt * 2 + dv) % 8
                    pieces = [((lambda k=k, tt=tt: Ofm[:, k, tt * 128:(tt + 1) * 128]),
                               (lambda k=k, dv=dv, wiv=wiv: wpool[wiv][:, k, dv * 512:(dv + 1) * 512]),
                               [wtv] + ve) for k in range(KC)]
                    b.mm_acc(bk, pieces, tag=f"vh{c}")
                    last_mm = b.bank_prod[bk]
                    vh_evs.append(b.consume("scalar", bk,
                                   lambda h, bk=bk, gt=gt, dv=dv: h.copy(
                                       vh[:, gt, dv * 512:(dv + 1) * 512],
                                       b.psum[bk][:]), tag=f"vh{c}ev"))
            wbusy[wiv] = last_mm

        o_evs = []
        def emit_qk(hh, qt):
            toks = []
            for kc in range(4):
                bk = kc
                pieces = [((lambda hh=hh, qt=qt: qh[:, hh, qt * 128:(qt + 1) * 128]),
                           (lambda hh=hh, kc=kc: kh[:, hh, kc * 512:(kc + 1) * 512]),
                           qh_evs + kh_evs)]
                b.mm_acc(bk, pieces, tag=f"qk{c}")
                toks.append(b.bank_prod[bk])
            return toks

        def emit_softmax(hh, qt, prev_pt):
            sb = []
            for kc in range(4):
                sb.append(b.consume("vector", kc,
                          lambda h, kc=kc: h.reduce_max(
                              nmax2[:, kc:kc + 1], b.psum[kc][:], axis=AX.X,
                              negate=True), tag=f"qkm{c}"))
            t_d = s.emit("vector", lambda h: h.drain(), after=sb, tag="qkmd")
            t_mx = s.emit("vector", lambda h: h.tensor_reduce(
                nmax2[:, 4:5], nmax2[:, 0:4], axis=AX.X, op=ALU.min),
                after=[t_d], tag="qkm2")
            t_mx = s.emit("vector", lambda h: h.drain(), after=[t_mx], tag="qkm2d")
            t_mx = s.emit("vector", lambda h: h.tensor_scalar(
                nmax2[:, 5:6], nmax2[:, 4:5], RSQ, None, op0=ALU.mult),
                after=[t_mx], tag="qkm3")
            t_mx = s.emit("vector", lambda h: h.drain(), after=[t_mx], tag="qkm3d")
            ex_toks = []
            for kc in range(4):
                aft = [t_mx] + ([prev_pt] if kc == 0 else [])
                ex_toks.append(b.consume("scalar", kc,
                               lambda h, kc=kc: h.activation(
                                   Pbuf[:, kc * 512:(kc + 1) * 512],
                                   b.psum[kc][:], AF.Exp,
                                   bias=nmax2[:, 5:6], scale=RSQ,
                                   accum_out=ssum[:, kc:kc + 1]),
                               after=aft, tag=f"ex{c}"))
            t_sm = s.emit("vector", lambda h: h.reduce_sum(
                ssum[:, 4:5], ssum[:, 0:4], axis=AX.X), after=ex_toks, tag="psm")
            t_sm = s.emit("vector", lambda h: h.drain(), after=[t_sm], tag="psmd")
            t_rc = s.emit("vector", lambda h: h.reciprocal(
                rcol[:], ssum[:, 4:5]), after=[t_sm], tag="prc")
            t_rc = s.emit("vector", lambda h: h.drain(), after=[t_rc], tag="prcd")
            t_pn = s.emit("vector", lambda h: h.tensor_scalar(
                Pbuf[:], Pbuf[:], rcol[:], None, op0=ALU.mult),
                after=[t_rc] + ex_toks, tag="pn")
            return t_pn

        def emit_ptpv(hh, qt, t_pn):
            pt_evs = []
            for grp in range(2):
                bkp = 4 + grp
                pieces = [((lambda ba, jj=jj: ba.rearrange("p (n m) -> p n m", m=128)[:, jj, :]),
                           (lambda kcc=grp * 8 + jj: Pbuf[:, kcc * 128:(kcc + 1) * 128]),
                           [t_pn]) for jj in range(8)]
                b.transpose_group(bkp, pieces, bf16=True, tag=f"pt{c}")
                pt_evs.append(b.consume("scalar", bkp,
                               lambda h, bkp=bkp, grp=grp: h.copy(
                                   PT[:, grp * 8:(grp + 1) * 8, :],
                                   b.psum[bkp][:].bitcast(BF16).rearrange(
                                       "p (n m) -> p n m", m=128)),
                               tag=f"ptev{c}"))
            pt_prod = b.bank_prod[5]
            bko = 6 if (hh * NT + qt) % 2 == 0 else 7
            pieces = [((lambda kc=kc, hh=hh: vh[:, kc, hh * 128:(hh + 1) * 128]),
                       (lambda kc=kc: PT[:, kc, :]),
                       pt_evs + vh_evs) for kc in range(NT)]
            b.mm_acc(bko, pieces, out_ap=b.psum[bko][:, 0:128], tag=f"pv{c}")
            o_evs.append(b.consume("scalar", bko,
                           lambda h, bko=bko, hh=hh, qt=qt: h.copy(
                               Ofm[:, hh, qt * 128:(qt + 1) * 128],
                               b.psum[bko][:, 0:128]),
                           after=([qspt] if (hh == 0 and qt == 0) else []),
                           tag=f"oev{c}"))
            return pt_prod

        iters = [(hh, qt) for hh in range(8) for qt in range(NT)]
        pend = None
        for hh, qt in iters:
            emit_qk(hh, qt)
            if pend is not None:
                prev_pt = emit_ptpv(pend[0], pend[1], pend[2])
            t_pn = emit_softmax(hh, qt, prev_pt)
            pend = (hh, qt, t_pn)
        prev_pt = emit_ptpv(pend[0], pend[1], pend[2])

        wio, wto = wload_pool(g["Wo0"][c], ("Wo0", c), tag=f"wo{c}w")
        if c < 5:
            comb_toks = []
            qjit_busy = [None, None]
            if ctx.get("p2_bar"):
                s.barrier()
            for tq in range(4):
                for oc in range(8):
                    bi = (tq * 8 + oc) % 2
                    qtok = s.emit("sync", lambda h, oc=oc, tq=tq, bi=bi: h.dma_start(
                        out=qjit[bi][:], in_=q_spill[oc, :, tq * 512:(tq + 1) * 512]),
                        after=[qjit_busy[bi], qspt], dma=True, tag=f"qre{c}")
                    ctx["q_read_done"] = qtok
                    oneed = [o_evs[hh2 * NT + qt2] for hh2 in range(8)
                             for qt2 in range(tq * 4, tq * 4 + 4)]
                    pieces = [((lambda k=k, oc=oc, wio=wio: wpool[wio][:, k, oc * 128:(oc + 1) * 128]),
                               (lambda k=k, tq=tq: Ofm[:, k, tq * 512:(tq + 1) * 512]),
                               [wto] + oneed) for k in range(KC)]
                    b.mm_acc(oc, pieces, tag=f"wo{c}")
                    nb = (tq * 8 + oc) % 2
                    tcomb = b.consume("vector", oc,
                                      lambda h, oc=oc, bi=bi, nb=nb, awc=awc:
                                      h.scalar_tensor_tensor(
                                          ncomb[nb][:], b.psum[oc][:], awc,
                                          qjit[bi][:], op0=ALU.mult, op1=ALU.add),
                                      after=[qtok, ncomb_busy[nb]], tag=f"cb{c}")
                    sp = s.emit("sync", lambda h, c=c, oc=oc, tq=tq, nb=nb:
                                h.dma_start(
                                    out=node_spill[c, oc, :, tq * 512:(tq + 1) * 512],
                                    in_=ncomb[nb][:]), after=[tcomb], dma=True,
                                tag=f"nsp{c}")
                    qjit_busy[bi] = tcomb
                    ncomb_busy[nb] = sp
                    comb_toks.append(sp)
                    slot_done[c] = sp
                if ctx.get("p2_bar"):
                    s.barrier()
            wbusy[wio] = comb_toks[-1]
            prev_comb = comb_toks[-1]
            if ctx.get("p2_bar"):
                s.barrier()
        else:
            tok_xfm2 = [None] * NT
            for tile in range(NT):
                wo_toks = []
                oneed5 = [o_evs[hh2 * NT + tile] for hh2 in range(8)]
                for dvh in range(2):
                    bk = dvh
                    pieces = [((lambda k=k, tile=tile: Ofm[:, k, tile * 128:(tile + 1) * 128]),
                               (lambda k=k, dvh=dvh, wio=wio: wpool[wio][:, k, dvh * 512:(dvh + 1) * 512]),
                               [wto] + oneed5) for k in range(KC)]
                    b.mm_acc(bk, pieces, tag="wo5")
                    wo_toks.append(b.bank_prod[bk])
                qjits = []
                for ch in range(KC):
                    bi = ch % 2
                    tok = s.emit("sync", lambda h, ch=ch, tile=tile, bi=bi:
                                 h.dma_start(out=qjit[bi][:, 0:128],
                                             in_=q_spill[ch, :, tile * 128:(tile + 1) * 128]),
                                 after=[ctx.get("p2_q5jb%d" % bi), qspt],
                                 dma=True, tag="q5j")
                    ctx["q_read_done"] = tok
                    qjits.append((bi, tok))
                    bkq = 2 + (ch % 2) * 1
                    bkq = 2 if ch < 4 else 4
                    waits = [tok]
                    if ch % 4 == 0 and b.bank_cons[bkq] is not None:
                        waits.append(b.bank_cons[bkq])
                    tok_p = s.emit("tensor", lambda h, bi=bi, ch=ch, bkq=bkq:
                                   h.transpose(b.psum[bkq][:].rearrange(
                                       "p (n m) -> p n m", m=128)[:, ch % 4, :],
                                       qjit[bi][:, 0:128], b.identf()),
                                   after=waits, tag="q5t")
                    ctx["p2_q5jb%d" % bi] = tok_p
                    b.bank_prod[bkq] = tok_p
                    if ch % 4 == 3:
                        b.consume("scalar", bkq, lambda h, bkq=bkq, ch=ch: h.copy(
                            q5tf[:, (ch - 3) * 128:(ch + 1) * 128],
                            b.psum[bkq][:]),
                            after=([prev_pt] if ch == 3 else []), tag="q5ev")
                t_q5 = s.last("scalar")
                cb_toks = []
                for dvh in range(2):
                    cb_toks.append(b.consume("vector", dvh,
                                   lambda h, dvh=dvh, awc=awc: h.scalar_tensor_tensor(
                                       n5tm[:, dvh * 512:(dvh + 1) * 512],
                                       b.psum[dvh][:], awc,
                                       q5tf[:, dvh * 512:(dvh + 1) * 512],
                                       op0=ALU.mult, op1=ALU.add),
                                   after=[t_q5], tag="n5c"))
                t_ln1 = b.spine_ln(n5tm[:], n5tm[:], cb_toks, eps=LN_EPS,
                                   tag=f"iln{tile}", scratch_guard=[])
                t_gb = s.emit("vector", lambda h: h.tensor_tensor(
                    n5tm[:], n5tm[:], lng_sb[:], op=ALU.mult),
                    after=[t_ln1, ct2], tag="gbm")
                t_gb = s.emit("vector", lambda h: h.tensor_tensor(
                    n5tm[:], n5tm[:], lnb_sb[:], op=ALU.add),
                    after=[t_gb], tag="gba")
                xt = xt_sb[tile % 2]
                tlx = s.emit("sync", lambda h, tile=tile, xt=xt: h.dma_start(
                    out=xt[:], in_=x_spill[tile]), after=[ctx.get("p2_xt_busy")],
                    dma=True, tag="x1re")
                t_add = s.emit("vector", lambda h, xt=xt: h.tensor_tensor(
                    x2t[:], n5tm[:], xt[:], op=ALU.add),
                    after=[t_gb, tlx] + ctx.get("p2_x2t_busy", []), tag="x2add")
                ctx["p2_xt_busy"] = t_add
                t_ln2 = b.spine_ln(x2t[:], x2t[:], [t_add], tag=f"ln2_{tile}",
                                   scratch_guard=[])
                spx = s.emit("sync", lambda h, tile=tile: h.dma_start(
                    out=x_spill[tile], in_=x2t[:]), after=[t_ln2], dma=True,
                    tag="x2sp")
                t_cast = s.emit("scalar", lambda h: h.copy(tmb[:], x2t[:]),
                                after=[t_ln2], tag="x2cast")
                ctx["p2_x2t_busy"] = [spx, t_cast]
                bk2 = 3
                pieces = [((lambda ba, ch=ch: ba.rearrange("p (n m) -> p n m", m=128)[:, ch, :]),
                           (lambda ch=ch: tmb[:, ch * 128:(ch + 1) * 128]),
                           [t_cast]) for ch in range(KC)]
                ctx["p2_tmb_busy"] = b.transpose_group(bk2, pieces, bf16=True, tag="x2tr")
                ev2 = b.consume("scalar", bk2,
                                lambda h, bk2=bk2: h.copy(
                                    qstage[:].bitcast(BF16).rearrange(
                                        "p (n m) -> p n m", m=128),
                                    b.psum[bk2][:].bitcast(BF16).rearrange(
                                        "p (n m) -> p n m", m=128)),
                                after=[tok_xfm2[tile - 1] if tile else None,
                                       qstage_busy[0]],
                                tag="x2ev")
                tok_xfm2[tile] = s.emit("sync", lambda h, tile=tile: h.dma_start(
                    out=node_spill[5].transpose((1, 0, 2))[:, :, tile * 128:(tile + 1) * 128],
                    in_=qstage[:].bitcast(BF16).rearrange(
                        "p (n m) -> p n m", m=128)), after=[ev2], dma=True,
                    tag="x2fmsp")
                qstage_busy[0] = tok_xfm2[tile]
                ctx.setdefault("tok_x2spill", []).append(spx)
            ctx["tok_x2fm"] = tok_xfm2
            slot_done[5] = tok_xfm2[-1]
    b.close_phase()
    ctx["p2_done"] = True


def p3(ctx):
    b = ctx["b"]; s = ctx["s"]; g = ctx["g"]
    stat = ctx["stat"]; tmb = ctx["tmb"]; xt_sb = ctx["xt_sb"]
    node_spill = ctx["node_spill"]; x_spill = ctx["x_spill"]
    RSQ64 = float(1.0 / np.sqrt(64.0))

    b.open_phase()
    qh1 = b.sbuf("qh1", [128, KC, T], BF16, phase=True)
    kh1 = b.sbuf("kh1", [128, KC, T], BF16, phase=True)
    vh1 = b.sbuf("vh1", [128, NT, D], BF16, phase=True)
    Ofm1 = b.sbuf("Ofm1", [128, KC, T], BF16, phase=True)
    wpool = [b.sbuf(f"wp3_{i}", [128, KC, D], BF16, phase=True) for i in range(2)]
    wbusy = [None, None]
    jit = [b.sbuf(f"jit3_{i}", [128, 512], BF16, phase=True) for i in range(10)]
    jit_busy = [None] * 10
    jit_rr = [0]
    Pbuf = b.sbuf("Pbuf3", [128, T], BF16, phase=True)
    PT = b.sbuf("PT3", [128, NT, 128], BF16, phase=True)
    nmax2 = b.sbuf("nmax3", [128, 8], F32, phase=True)
    ssum = b.sbuf("ssum3", [128, 8], F32, phase=True)
    rcol = b.sbuf("rcol3", [128, 1], F32, phase=True)
    bcol = b.sbuf("bcol3", [128, 8], F32, phase=True)
    bo1_sb = b.sbuf("bo1_sb3", [128, D], F32, phase=True)
    mask_sb = b.sbuf("mask_sb3", [128, 128], F32, phase=True)
    x3r = b.sbuf("x3r", [128, D], F32, phase=True)
    x3fmt = b.sbuf("x3fmt", [128, KC, 128], BF16, phase=True)

    c1 = s.emit("sync", lambda h: h.dma_start(out=bo1_sb[:], in_=g["bo1_rep"]),
                dma=True, tag="bo1l")
    c2 = s.emit("sync", lambda h: h.dma_start(out=mask_sb[:], in_=g["mask128"]),
                dma=True, tag="maskl")

    def jload(src_ap, after=(), tag="j3", width=512):
        bi = jit_rr[0]
        jit_rr[0] = (bi + 1) % 10
        tok = s.emit("sync", lambda h, src_ap=src_ap, bi=bi, width=width:
                     h.dma_start(out=jit[bi][:, 0:width], in_=src_ap),
                     after=[jit_busy[bi]] + list(after), dma=True, tag=tag)
        return bi, tok

    def wload(name, slot, after=(), tag="w3"):
        toks = []
        for k in range(KC):
            toks.append(s.emit("gpsimd", lambda h, k=k, slot=slot, name=name:
                               h.dma_start(out=wpool[slot][:, k, :],
                                           in_=g[name][k * 128:(k + 1) * 128, :]),
                               after=[wbusy[slot]] + list(after), dma=True,
                               tag=tag))
        return toks[-1]

    def proj1(wname, bname, dst, slot, tag):
        wt = wload(wname, slot, tag=tag + "w")
        bl = s.emit("sync", lambda h: h.dma_start(
            out=bcol[:], in_=g[bname]), after=[ctx.get("p3_bcol_busy")],
            dma=True, tag=tag + "b")
        evs = []
        last_mm = None
        for tq in range(4):
            bufs = [jload(node_spill[5, k, :, tq * 512:(tq + 1) * 512],
                          tag=tag + "j") for k in range(KC)]
            for oc in range(8):
                pieces = []
                for k in range(KC):
                    bi, ltok = bufs[k]
                    pieces.append((
                        (lambda k=k, oc=oc, slot=slot: wpool[slot][:, k, oc * 128:(oc + 1) * 128]),
                        (lambda bi=bi: jit[bi][:]), [wt, ltok, bl]))
                b.mm_acc(oc, pieces, tag=tag)
                last_mm = b.bank_prod[oc]
            for bi, _ in bufs:
                jit_busy[bi] = last_mm
            for oc in range(8):
                evs.append(b.consume("scalar", oc,
                           lambda h, oc=oc, tq=tq: h.activation(
                               dst(oc, tq), b.psum[oc][:], AF.Identity,
                               bias=bcol[:, oc:oc + 1], scale=1.0),
                           tag=tag + "ev"))
        wbusy[slot] = last_mm
        ctx["p3_bcol_busy"] = evs[-1]
        return evs

    qh_evs = proj1("Wq1", "bq1", lambda oc, tq: qh1[:, oc, tq * 512:(tq + 1) * 512],
                   0, "q1")
    kh_evs = proj1("Wk1", "bk1", lambda oc, tq: kh1[:, oc, tq * 512:(tq + 1) * 512],
                   1, "k1")
    vf_evs = proj1("Wv1", "bv1", lambda oc, tq: Ofm1[:, oc, tq * 512:(tq + 1) * 512],
                   0, "v1")
    vh_evs = []
    for t in range(NT):
        bk = b.bank()
        pieces = [((lambda ba, c=c: ba.rearrange("p (n m) -> p n m", m=128)[:, c, :]),
                   (lambda c=c, t=t: Ofm1[:, c, t * 128:(t + 1) * 128]),
                   vf_evs) for c in range(KC)]
        b.transpose_group(bk, pieces, bf16=True, tag="vtr")
        vh_evs.append(b.consume("scalar", bk, lambda h, bk=bk, t=t: h.copy(
            vh1[:, t, :], b.psum[bk][:].bitcast(BF16)), tag="vtrev"))

    prev_pt = None
    o_evs = {}

    def emit_qk1(hh, qt):
        ch, base = hh // 2, (hh % 2) * 64
        nfull = (qt * 128) // 512
        rem = qt * 128 - nfull * 512
        used = []
        for kc in range(nfull):
            bk = kc % 3
            pieces = [((lambda ch=ch, base=base, qt=qt:
                        qh1[base:base + 64, ch, qt * 128:(qt + 1) * 128]),
                       (lambda ch=ch, base=base, kc=kc:
                        kh1[base:base + 64, ch, kc * 512:(kc + 1) * 512]),
                       qh_evs + kh_evs)]
            b.mm_acc(bk, pieces, tag="qk1")
            used.append((bk, kc * 512, 512))
        if rem:
            pieces = [((lambda ch=ch, base=base, qt=qt:
                        qh1[base:base + 64, ch, qt * 128:(qt + 1) * 128]),
                       (lambda ch=ch, base=base, nfull=nfull, rem=rem:
                        kh1[base:base + 64, ch, nfull * 512:nfull * 512 + rem]),
                       qh_evs + kh_evs)]
            b.mm_acc(3, pieces, out_ap=b.psum[3][:, 0:rem], tag="qk1r")
            used.append((3, nfull * 512, rem))
        pieces = [((lambda ch=ch, base=base, qt=qt:
                    qh1[base:base + 64, ch, qt * 128:(qt + 1) * 128]),
                   (lambda ch=ch, base=base, qt=qt:
                    kh1[base:base + 64, ch, qt * 128:(qt + 1) * 128]),
                   qh_evs + kh_evs)]
        b.mm_acc(4, pieces, out_ap=b.psum[4][:, 0:128], tag="qk1d")
        t_msk = b.consume("vector", 4, lambda h: h.scalar_tensor_tensor(
            b.psum[4][:, 0:128], mask_sb[:], float(np.sqrt(64.0)),
            b.psum[4][:, 0:128], op0=ALU.mult, op1=ALU.add),
            after=[c2], tag="mask1")
        used.append((4, qt * 128, 128))
        return used, t_msk

    def emit_sm1(hh, qt, used, t_msk, prev_pt):
        sb = []
        for i, (bk, off, wid) in enumerate(used):
            aft = [t_msk] if bk == 4 else []
            sb.append(b.consume("vector", bk, lambda h, bk=bk, wid=wid, i=i:
                      h.reduce_max(nmax2[:, i:i + 1], b.psum[bk][:, 0:wid],
                                   axis=AX.X, negate=True),
                      after=aft, tag="qk1m"))
        nb = len(used)
        t_d = s.emit("vector", lambda h: h.drain(), after=sb, tag="m1d")
        t_mx = s.emit("vector", lambda h, nb=nb: h.tensor_reduce(
            nmax2[:, 6:7], nmax2[:, 0:nb], axis=AX.X, op=ALU.min),
            after=[t_d], tag="m1")
        t_mx = s.emit("vector", lambda h: h.drain(), after=[t_mx], tag="m1dd")
        t_mx = s.emit("vector", lambda h: h.tensor_scalar(
            nmax2[:, 7:8], nmax2[:, 6:7], RSQ64, None, op0=ALU.mult),
            after=[t_mx], tag="m1s")
        t_mx = s.emit("vector", lambda h: h.drain(), after=[t_mx], tag="m1sd")
        ex_toks = []
        for i, (bk, off, wid) in enumerate(used):
            aft = [t_mx] + ([prev_pt] if i == 0 else [])
            ex_toks.append(b.consume("scalar", bk,
                           lambda h, bk=bk, off=off, wid=wid, i=i: h.activation(
                               Pbuf[:, off:off + wid], b.psum[bk][:, 0:wid],
                               AF.Exp, bias=nmax2[:, 7:8], scale=RSQ64,
                               accum_out=ssum[:, i:i + 1]),
                           after=aft, tag="ex1"))
        t_sm = s.emit("vector", lambda h, nb=nb: h.reduce_sum(
            ssum[:, 6:7], ssum[:, 0:nb], axis=AX.X), after=ex_toks, tag="ps1")
        t_sm = s.emit("vector", lambda h: h.drain(), after=[t_sm], tag="ps1d")
        t_rc = s.emit("vector", lambda h: h.reciprocal(
            rcol[:], ssum[:, 6:7]), after=[t_sm], tag="rc1")
        t_rc = s.emit("vector", lambda h: h.drain(), after=[t_rc], tag="rc1d")
        valid = qt * 128 + 128
        t_pn = s.emit("vector", lambda h, valid=valid: h.tensor_scalar(
            Pbuf[:, 0:valid], Pbuf[:, 0:valid], rcol[:], None, op0=ALU.mult),
            after=[t_rc] + ex_toks, tag="pn1")
        return t_pn

    def emit_pv1(hh, qt, t_pn):
        ch, base = hh // 2, (hh % 2) * 64
        ntile = qt + 1
        pt_evs = []
        lastp = None
        for grp in range((ntile + 7) // 8):
            bkp = 5 + grp
            cnt = min(8, ntile - grp * 8)
            pieces = [((lambda ba, jj=jj: ba.rearrange(
                        "p (n m) -> p n m", m=128)[:, jj, :]),
                       (lambda kcc=grp * 8 + jj: Pbuf[:, kcc * 128:(kcc + 1) * 128]),
                       [t_pn]) for jj in range(cnt)]
            b.transpose_group(bkp, pieces, bf16=True, tag="pt1")
            pt_evs.append(b.consume("scalar", bkp,
                           lambda h, bkp=bkp, grp=grp, cnt=cnt: h.copy(
                               PT[:, grp * 8:grp * 8 + cnt, :],
                               b.psum[bkp][:].bitcast(BF16).rearrange(
                                   "p (n m) -> p n m", m=128)[:, 0:cnt, :]),
                           tag="pt1ev"))
            lastp = b.bank_prod[bkp]
        pieces = [((lambda kc=kc, hh=hh: vh1[:, kc, hh * 64:(hh + 1) * 64]),
                   (lambda kc=kc: PT[:, kc, :]),
                   pt_evs + vh_evs) for kc in range(ntile)]
        b.mm_acc(7, pieces, out_ap=b.psum[7][base:base + 64, 0:128], tag="pv1")
        o_evs[(hh, qt)] = b.consume("scalar", 7,
                       lambda h, ch=ch, base=base, qt=qt: h.copy(
                           Ofm1[base:base + 64, ch, qt * 128:(qt + 1) * 128],
                           b.psum[7][base:base + 64, 0:128]),
                       after=(vh_evs if (hh == 0 and qt == 0) else []),
                       tag="oev1")
        return lastp

    pend = None
    for hh in range(16):
        for qt in range(NT):
            used, t_msk = emit_qk1(hh, qt)
            if pend is not None:
                prev_pt = emit_pv1(pend[0], pend[1], pend[2])
            t_pn = emit_sm1(hh, qt, used, t_msk, prev_pt)
            pend = (hh, qt, t_pn)
    prev_pt = emit_pv1(pend[0], pend[1], pend[2])
    o_evs = [o_evs[(hh, qt)] for hh in range(16) for qt in range(NT)]

    wt_o = wload("Wo1", 0, tag="wo1w")
    tok_x3fm = [None] * NT
    for tile in range(NT):
        wo_toks = []
        oneed = [o_evs[hh2 * NT + tile] for hh2 in range(16)]
        for dv in range(2):
            bk = dv
            pieces = [((lambda k=k, tile=tile: Ofm1[:, k, tile * 128:(tile + 1) * 128]),
                       (lambda k=k, dv=dv: wpool[0][:, k, dv * 512:(dv + 1) * 512]),
                       [wt_o] + oneed) for k in range(KC)]
            b.mm_acc(bk, pieces, tag="wo1")
            wo_toks.append(b.bank_prod[bk])
        xt = xt_sb[tile % 2]
        tlx = s.emit("sync", lambda h, tile=tile, xt=xt: h.dma_start(
            out=xt[:], in_=x_spill[tile]), after=[ctx.get("p3_xt_busy")],
            dma=True, tag="x2re")
        add_toks = []
        for dv in range(2):
            t1 = b.consume("vector", dv, lambda h, dv=dv: h.scalar_tensor_tensor(
                x3r[:, dv * 512:(dv + 1) * 512], b.psum[dv][:], 1.0,
                bo1_sb[:, dv * 512:(dv + 1) * 512], op0=ALU.mult, op1=ALU.add),
                after=[c1] + ctx.get("p3_x3r_busy", []), tag="bo1a")
            add_toks.append(s.emit("vector", lambda h, dv=dv, xt=xt: h.tensor_tensor(
                x3r[:, dv * 512:(dv + 1) * 512], x3r[:, dv * 512:(dv + 1) * 512],
                xt[:, dv * 512:(dv + 1) * 512], op=ALU.add),
                after=[t1, tlx], tag="x3add"))
        ctx["p3_xt_busy"] = add_toks[-1]
        t_ln = b.spine_ln(x3r[:], x3r[:], add_toks, tag=f"ln3_{tile}")
        spx = s.emit("sync", lambda h, tile=tile: h.dma_start(
            out=x_spill[tile], in_=x3r[:]), after=[t_ln], dma=True, tag="x3sp")
        t_cast = s.emit("scalar", lambda h: h.copy(tmb[:], x3r[:]),
                        after=[t_ln], tag="x3cast")
        ctx["p3_x3r_busy"] = [spx, t_cast]
        bk2 = 3
        pieces = [((lambda ba, chn=chn: ba.rearrange("p (n m) -> p n m", m=128)[:, chn, :]),
                   (lambda chn=chn: tmb[:, chn * 128:(chn + 1) * 128]),
                   [t_cast, ctx.get("p3_tmb_busy")]) for chn in range(KC)]
        ctx["p3_tmb_busy"] = b.transpose_group(bk2, pieces, bf16=True, tag="x3tr")
        ev2 = b.consume("scalar", bk2, lambda h, bk2=bk2: h.copy(
            x3fmt[:], b.psum[bk2][:].bitcast(BF16).rearrange(
                "p (n m) -> p n m", m=128)),
            after=[tok_x3fm[tile - 1] if tile else None], tag="x3ev")
        tok_x3fm[tile] = s.emit("sync", lambda h, tile=tile: h.dma_start(
            out=node_spill[4].transpose((1, 0, 2))[:, :, tile * 128:(tile + 1) * 128],
            in_=x3fmt[:]), after=[ev2], dma=True, tag="x3fmsp")
        ctx.setdefault("tok_x3spill", []).append(spx)
    ctx["tok_x3fm"] = tok_x3fm
    if "p3t" in b.dbg_taps:
        for nm, ap_, shp in (("qh1", qh1, (128, KC, T)), ("kh1", kh1, (128, KC, T)),
                             ("vh1", vh1, (128, NT, D)), ("O1", Ofm1, (128, KC, T))):
            o_ = b.dout("dbg_" + nm, shp, BF16)
            b.dbg_tokens.append(s.emit("sync", lambda h, o_=o_, ap_=ap_:
                h.dma_start(out=o_, in_=ap_[:]), after=o_evs + [tok_x3fm[-1]],
                dma=True, tag="dbg"))
    b.close_phase()
    ctx["p3_done"] = True


def p4(ctx):
    b = ctx["b"]; s = ctx["s"]; g = ctx["g"]
    xt_sb = ctx["xt_sb"]; x_spill = ctx["x_spill"]
    node_spill = ctx["node_spill"]; h_spill = ctx["h_spill"]
    y_out = ctx["y_out"]

    b.open_phase()
    wy_sb = b.sbuf("wy_sb", [128, 32, 128], BF16, phase=True)
    wz_sb = b.sbuf("wz_sb", [128, 64, 128], BF16, phase=True)
    ybuf = b.sbuf("ybuf", [128, 32, 128], BF16, phase=True)
    hstage = [b.sbuf(f"hstage{i}", [128, 8, 128], BF16, phase=True) for i in range(2)]
    hstage_busy = [None, None]
    jit4 = [b.sbuf(f"jit4_{i}", [128, 128], BF16, phase=True) for i in range(10)]
    jit4_busy = [None] * 10
    jr = [0]

    tok_wy = []
    for yc in range(32):
        tok_wy.append(s.emit("gpsimd", lambda h, yc=yc: h.dma_start(
            out=wy_sb[:, yc, :], in_=g["Wyp"].rearrange(
                "a b p m -> (a b) p m")[yc]), dma=True, tag="wyl"))
    tok_wz = []
    for t in range(64):
        tok_wz.append(s.emit("gpsimd", lambda h, t=t: h.dma_start(
            out=wz_sb[:, t, :], in_=g["Wzp"][t]), dma=True, tag="wzl"))

    for tile in range(NT):
        xj = []
        for ch in range(KC):
            bi = jr[0]
            jr[0] = (bi + 1) % 10
            tok = s.emit("sync", lambda h, ch=ch, tile=tile, bi=bi: h.dma_start(
                out=jit4[bi][:], in_=node_spill[4, ch, :, tile * 128:(tile + 1) * 128]),
                after=[jit4_busy[bi], ctx["tok_x3fm"][tile]], dma=True, tag="x3j")
            xj.append((bi, tok))
        yev = []
        last_mm = None
        for hh in range(4):
            for rc in range(KC):
                yc = hh * 8 + rc
                bk = yc % 6
                bi, ltok = xj[rc]
                pieces = [((lambda yc=yc: wy_sb[:, yc, :]),
                           (lambda bi=bi: jit4[bi][:]),
                           [tok_wy[yc], ltok])]
                b.mm_acc(bk, pieces, out_ap=b.psum[bk][:, 0:128], tag="ymm")
                last_mm = b.bank_prod[bk]
                yev.append(b.consume("scalar", bk, lambda h, bk=bk, yc=yc: h.copy(
                    ybuf[:, yc, :], b.psum[bk][:, 0:128]), tag="yev"))
        for bi, _ in xj:
            jit4_busy[bi] = last_mm
        for grp in range(16):
            hs = grp % 2
            zev = []
            for gg in range(8):
                zc = grp * 8 + gg
                hh, p = zc // 32, zc % 32
                tz, zbase = ZMAP[(hh, p)]
                q = p // 2
                ybase = (q % 2) * 64
                ychunk = hh * 8 + q // 2
                bk = 6 + (gg % 2)
                pieces = [((lambda tz=tz, zbase=zbase: wz_sb[zbase:zbase + 64, tz, :]),
                           (lambda ybase=ybase, ychunk=ychunk:
                            ybuf[ybase:ybase + 64, ychunk, :]),
                           [tok_wz[tz]] + yev)]
                b.mm_acc(bk, pieces, out_ap=b.psum[bk][:, 0:128], tag="zmm")
                zev.append(b.consume("scalar", bk,
                           lambda h, bk=bk, gg=gg, hs=hs: h.activation(
                               hstage[hs][:, gg, :], b.psum[bk][:, 0:128],
                               AF.Gelu),
                           after=([hstage_busy[hs]] if gg == 0 else []),
                           tag="gelu"))
            sp = s.emit("sync", lambda h, tile=tile, grp=grp, hs=hs: h.dma_start(
                out=h_spill[tile, grp * 8:(grp + 1) * 8].transpose((1, 0, 2)),
                in_=hstage[hs][:]), after=zev, dma=True, tag="hsp")
            hstage_busy[hs] = sp
            ctx.setdefault("tok_hsp", []).append(sp)
    b.close_phase()

    b.open_phase()
    wc_sb = b.sbuf("wc_sb", [128, 128, 512], BF16, phase=True)
    hjit = [b.sbuf(f"hjit{i}", [128, 8, 128], BF16, phase=True) for i in range(3)]
    hjit_busy = [None] * 3
    x4stash = b.sbuf("x4stash", [128, NT, 512], F32, phase=True)
    x4r = b.sbuf("x4r", [128, D], F32, phase=True)
    hsp_done = ctx["tok_hsp"][-1]

    for dh in range(2):
        tok_wc = []
        for zc in range(128):
            tok_wc.append(s.emit("gpsimd", lambda h, zc=zc, dh=dh: h.dma_start(
                out=wc_sb[:, zc, :],
                in_=g["WcT"][zc * 128:(zc + 1) * 128, dh * 512:(dh + 1) * 512]),
                after=[ctx.get("p4_wc_busy")], dma=True, tag="wcl"))
        for tile in range(NT):
            bk = tile % 2
            last_mm = None
            mm_toks = []
            for zg in range(16):
                hb = zg % 3
                htok = s.emit("sync", lambda h, tile=tile, zg=zg, hb=hb: h.dma_start(
                    out=hjit[hb][:], in_=h_spill[tile, zg * 8:(zg + 1) * 8]
                    .transpose((1, 0, 2))), after=[hjit_busy[hb], hsp_done],
                    dma=True, tag="hj")
                for gg in range(8):
                    zc = zg * 8 + gg
                    start = (zc == 0)
                    stop = (zc == 127)
                    waits = [htok, tok_wc[zc]]
                    if start and b.bank_cons[bk] is not None:
                        waits.append(b.bank_cons[bk])
                    last_mm = s.emit("tensor", lambda h, bk=bk, hb=hb, gg=gg,
                                     zc=zc, s0=start, s1=stop:
                                     h.matmul(b.psum[bk][:], hjit[hb][:, gg, :],
                                              wc_sb[:, zc, :], start=s0, stop=s1),
                                     after=waits, tag="cpmm")
                hjit_busy[hb] = last_mm
            b.bank_prod[bk] = last_mm
            mm_toks.append(last_mm)
            if dh == 0:
                ctx.setdefault("x4st_toks", {})[tile] = b.consume(
                    "scalar", bk, lambda h, bk=bk, tile=tile: h.copy(
                        x4stash[:, tile, :], b.psum[bk][:]), tag="x4st")
            else:
                xt = xt_sb[tile % 2]
                tlx = s.emit("sync", lambda h, tile=tile, xt=xt: h.dma_start(
                    out=xt[:], in_=x_spill[tile]), after=[ctx.get("p4_xt_busy")],
                    dma=True, tag="x3re")
                t1 = b.consume("vector", bk, lambda h, bk=bk, tile=tile, xt=xt:
                               h.tensor_tensor(x4r[:, 512:1024], b.psum[bk][:],
                                               xt[:, 512:1024], op=ALU.add),
                               after=[tlx] + ctx.get("p4_x4r_busy", []),
                               tag="x4a1")
                t2 = s.emit("vector", lambda h, tile=tile, xt=xt: h.tensor_tensor(
                    x4r[:, 0:512], x4stash[:, tile, :], xt[:, 0:512], op=ALU.add),
                    after=[t1, ctx["x4st_toks"][tile]], tag="x4a2")
                ctx["p4_xt_busy"] = t2
                t_ln = b.spine_ln(x4r[:], x4r[:], [t2], tag=f"ln4_{tile}")
                oy = s.emit("sync", lambda h, tile=tile: h.dma_start(
                    out=ctx["y_out"].rearrange("(n p) d -> n p d", p=128)[tile],
                    in_=x4r[:]), after=[t_ln], dma=True, tag="yout")
                ctx["p4_x4r_busy"] = [oy]
                b.out_toks.append(oy)
        ctx["p4_wc_busy"] = mm_toks[-1]
    b.close_phase()
    ctx["p4_done"] = True


def kernel(**inputs):
    ws = prep_host(inputs)
    x_full = np.ascontiguousarray(inputs["L0_attn_x"], dtype=np.float32)
    b, ctx = build()
    in_maps = []
    for core in range(8):
        m = {"x": x_full[core]}
        m.update(ws)
        in_maps.append(m)
    res = run_bass_kernel_spmd(b.nc, in_maps, core_ids=list(range(8)))
    return np.stack([r["y"] for r in res.results], axis=0)
